# version 04 (final)
# baseline (speedup 1.0000x reference)
import math
import numpy as np
import ml_dtypes
from contextlib import ExitStack

import concourse.bass as bass
import concourse.bacc as bacc_mod
import concourse.mybir as mybir
from concourse.bass_utils import run_bass_kernel_spmd

F32 = mybir.dt.float32
BF16 = mybir.dt.bfloat16
I32 = mybir.dt.int32
AF = mybir.ActivationFunctionType
OP = mybir.AluOpType
AX = mybir.AxisListType

D, DC1, DC, DR, H, DH = 4096, 1536, 512, 64, 32, 128
INTER, RANK = 11008, 1024
ND, NR, NI, NC1, NCC = D // 128, RANK // 128, INTER // 128, DC1 // 128, DC // 128
SC = 1.0 / math.sqrt(DH + DR)
G = 512


class _Inst:
    __slots__ = ("meth", "args", "kwargs", "incs")

    def __init__(self, meth, args, kwargs):
        self.meth, self.args, self.kwargs = meth, args, kwargs
        self.incs = []

    def then_inc(self, sem, n):
        self.incs.append((sem, n))
        return self


def _rs(a):
    return a() if callable(a) else a


class EngRec:
    def __init__(self, name):
        self.name = name
        self.ops = []

    def q(self, meth, *args, **kwargs):
        op = _Inst(meth, args, kwargs)
        self.ops.append(op)
        return op

    def raw(self, fn):
        self.ops.append(_Inst("__raw__", (fn,), {}))

    def replay(self, eng, regs):
        for op in self.ops:
            if op.meth == "__raw__":
                op.args[0](eng, regs)
                continue
            args = [_rs(a) for a in op.args]
            kwargs = {k: _rs(v) for k, v in op.kwargs.items()}
            inst = getattr(eng, op.meth)(*args, **kwargs)
            for sem, n in op.incs:
                inst.then_inc(sem, n)


class Sem:
    def __init__(self, nc, ctx, name):
        self.h = ctx.enter_context(nc.semaphore(name))
        self.n = 0

    def inc(self, k=1):
        self.n += k
        return self.h


def T3(dst2d, src2d, c):
    f = src2d.shape[-1]
    return (dst2d.rearrange("p (c f) -> p c f", c=c),
            src2d.rearrange("(c p) f -> p c f", p=128))


def MM(t, out_ap, lhsT_ap, rhs_ap, start, stop):
    N = out_ap.shape[-1]
    for o in range(0, N, 512):
        e_ = min(N, o + 512)
        t.q("matmul", out_ap[:, o:e_], lhsT=lhsT_ap, rhs=rhs_ap[:, o:e_],
            start=start, stop=stop)


def build_program(TH, n_phases=99, dev_dump=False):
    S, TQ = 2 * TH, TH // 2
    NKC = S // 128
    nc = bacc_mod.Bacc("TRN2", target_bir_lowering=False)
    REGS = {}

    ap = {}
    ap["xh"] = nc.dram_tensor("xh", [TH, D], F32, kind="ExternalInput").ap()
    ap["nonce"] = nc.dram_tensor("nonce", [1, 4], F32, kind="ExternalInput").ap()
    ap["bcols"] = nc.dram_tensor("bcols", [128, 269], F32, kind="ExternalInput").ap()
    for n, shp in [
        ("w_dq", [D, DC1]), ("w_uq", [DC1, D]), ("w_dkv", [D, DC]),
        ("w_uk", [DC, D]), ("w_uv", [DC, D]), ("w_rq", [DC1, H * DR]),
        ("w_rk", [D, DR]), ("w_out", [D, D]), ("b_uv", [1, D]),
        ("b_out", [1, D]), ("b2", [1, D]),
        ("U1", [D, RANK]), ("V1", [RANK, 2 * INTER]),
        ("U2", [INTER, RANK]), ("V2", [RANK, D]),
    ]:
        ap[n] = nc.dram_tensor(n, shp, F32, kind="ExternalInput").ap()
    out_d = nc.dram_tensor("out", [TQ, D], F32, kind="ExternalOutput").ap()

    K_sh = nc.dram_tensor("K_sh", [D, S], BF16, addr_space="Shared")
    V_sh = nc.dram_tensor("V_sh", [S, D], BF16, addr_space="Shared")
    Kr_sh = nc.dram_tensor("Kr_sh", [DR, S], BF16, addr_space="Shared")
    flags = nc.dram_tensor("flags", [1, 2], F32, addr_space="Shared")
    mx_sh = nc.dram_tensor("mx_sh", [2, 64], F32, addr_space="Shared")
    ln1_d = nc.dram_tensor("ln1_d", [TH, D], BF16)
    ln2_d = nc.dram_tensor("ln2_d", [TH, D], BF16)
    ln3_d = nc.dram_tensor("ln3_d", [TQ, D], BF16)
    x1q_d = nc.dram_tensor("x1q_d", [TQ, D], F32)
    x2r_d = nc.dram_tensor("x2r_d", [TQ, D], F32)
    x2_d = nc.dram_tensor("x2_d", [TQ, D], F32)
    O_d = nc.dram_tensor("O_d", [D, TQ], BF16)
    CQ_d = nc.dram_tensor("CQ_d", [128, NC1 * TQ], BF16)
    K_loc = nc.dram_tensor("K_loc", [D, TH], BF16)
    V_loc = nc.dram_tensor("V_loc", [TH, D], BF16)
    Kr_loc = nc.dram_tensor("Kr_loc", [DR, TH], BF16)
    U1_16 = nc.dram_tensor("U1_16", [D, RANK], BF16)
    V1_16 = nc.dram_tensor("V1_16", [RANK, 2 * INTER], BF16)
    U2_16 = nc.dram_tensor("U2_16", [INTER, RANK], BF16)
    V2_16 = nc.dram_tensor("V2_16", [RANK, D], BF16)

    dbg = {}
    ffd = {}
    if dev_dump:
        for n, shp in [("d_pt", [128, NR * G]), ("d_act", [128, NI * G]),
                       ("d_zt", [128, NR * G])]:
            ffd[n] = nc.dram_tensor(n, shp, BF16, kind="ExternalOutput").ap()
    if dev_dump:
        for n, src, shp, dt in [
            ("d_ln1", ln1_d, [TH, D], BF16), ("d_ln2", ln2_d, [TH, D], BF16),
            ("d_x1q", x1q_d, [TQ, D], F32), ("d_K", K_sh, [D, S], BF16),
            ("d_V", V_sh, [S, D], BF16), ("d_Kr", Kr_sh, [DR, S], BF16),
            ("d_cq", CQ_d, [128, NC1 * TQ], BF16),
            ("d_O", O_d, [D, TQ], BF16), ("d_x2r", x2r_d, [TQ, D], F32),
            ("d_x2", x2_d, [TQ, D], F32),
        ]:
            dbg[n] = (nc.dram_tensor(n, shp, dt, kind="ExternalOutput").ap(), src, shp)

    ctx = ExitStack()
    e = ctx.enter_context

    bia = e(nc.sbuf_tensor("bia", [128, 269], F32))
    bia_s = e(nc.sbuf_tensor("bia_s", [128, 48], F32))
    CB = {}
    off = 0
    for nm, w in [("dq", 12), ("uq", 32), ("dkv", 4), ("uk", 32), ("rq", 16),
                  ("rk", 1), ("b1g", 86), ("b1u", 86)]:
        CB[nm] = off
        off += w
    ones_bf = e(nc.sbuf_tensor("ones_bf", [128, 1], BF16))
    ones_f32 = e(nc.sbuf_tensor("ones_f32", [128, 1], F32))
    onesr_bf = e(nc.sbuf_tensor("onesr_bf", [1, 128], BF16))
    onesr_f32 = e(nc.sbuf_tensor("onesr_f32", [1, 128], F32))
    brow = e(nc.sbuf_tensor("brow", [1, 2 * D], BF16))
    buv_row = e(nc.sbuf_tensor("buv_row", [1, D], BF16))
    nonce_sb = e(nc.sbuf_tensor("nonce_sb", [1, 4], F32))
    nonce_i = e(nc.sbuf_tensor("nonce_i", [1, 4], I32))
    flag_sb = e(nc.sbuf_tensor("flag_sb", [1, 2], F32))
    flag_i = e(nc.sbuf_tensor("flag_i", [1, 2], I32))

    PSA = e(nc.psum_tensor("PSA", [128, 2048], F32))
    PSB = e(nc.psum_tensor("PSB", [128, 2048], F32))

    mxacc = e(nc.sbuf_tensor("mxacc", [1, 64], F32))
    sems = {}
    for sn in ["dW", "dT", "dX", "dS", "dO", "pe", "ve", "se", "po", "bar",
               "bw", "ph"]:
        sems[sn] = Sem(nc, ctx, sn)
    sU1 = [Sem(nc, ctx, f"sU1{i}") for i in range(2)]
    sV1 = [Sem(nc, ctx, f"sV1{i}") for i in range(2)]
    sU2 = [Sem(nc, ctx, f"sU2{i}") for i in range(2)]
    sV2 = [Sem(nc, ctx, f"sV2{i}") for i in range(2)]
    sLN = [Sem(nc, ctx, f"sLN{i}") for i in range(6)]
    sXT = Sem(nc, ctx, "sXT")
    sUK = [Sem(nc, ctx, f"sUK{i}") for i in range(2)]
    sKT = [Sem(nc, ctx, f"sKT{i}") for i in range(2)]
    sWO = [Sem(nc, ctx, f"sWO{i}") for i in range(2)]
    sPC = [Sem(nc, ctx, f"sPC{i}") for i in range(2)]
    sWS = [Sem(nc, ctx, f"sWS{i}") for i in range(2)]
    sRK = Sem(nc, ctx, "sRK")
    sWQ = [Sem(nc, ctx, f"sWQ{i}") for i in range(2)]
    dW, dT, dX, dS, dO = (sems[k] for k in ["dW", "dT", "dX", "dS", "dO"])
    pe, ve, se, po_s, ph = (sems[k] for k in ["pe", "ve", "se", "po", "ph"])

    class Rr:
        pass

    r = Rr()
    r.t, r.v, r.s, r.g, r.y = (EngRec(x) for x in "tvsgy")
    ALL = (r.t, r.v, r.s, r.g, r.y)

    def join():
        for eng in ALL:
            eng.q("nop").then_inc(ph.inc(), 1)
        tgt = ph.n
        for eng in ALL:
            eng.q("wait_ge", ph.h, tgt)

    def ln_pass(src_f32, dst_bf, sqt, mrow, nrow):
        r.v.q("reduce_sum", out=mrow[:, 0:1], in_=src_f32,
              axis=AX.X).then_inc(ve.inc(), 1)
        r.s.q("wait_ge", ve.h, ve.n)
        r.s.q("square", out=sqt[:, :], in_=src_f32).then_inc(se.inc(), 1)
        r.v.q("wait_ge", se.h, se.n)
        r.v.q("reduce_sum", out=mrow[:, 1:2], in_=sqt[:, :], axis=AX.X)
        r.v.q("tensor_scalar", out=nrow[:, 0:1], in0=mrow[:, 0:1],
              scalar1=1.0 / D, scalar2=None, op0=OP.mult).then_inc(ve.inc(), 1)
        r.v.q("wait_ge", ve.h, ve.n)
        r.v.q("tensor_tensor", out=nrow[:, 1:2], in0=nrow[:, 0:1],
              in1=nrow[:, 0:1], op=OP.mult)
        r.v.q("tensor_scalar", out=nrow[:, 2:3], in0=mrow[:, 1:2],
              scalar1=1.0 / D, scalar2=None, op0=OP.mult).then_inc(ve.inc(), 1)
        r.v.q("wait_ge", ve.h, ve.n)
        r.v.q("scalar_tensor_tensor", out=nrow[:, 3:4], in0=nrow[:, 2:3],
              scalar=1e-5, in1=nrow[:, 1:2], op0=OP.add,
              op1=OP.subtract).then_inc(ve.inc(), 1)
        r.s.q("wait_ge", ve.h, ve.n)
        r.s.q("activation", out=nrow[:, 4:5], in_=nrow[:, 3:4],
              func=AF.Ln).then_inc(se.inc(), 1)
        r.s.q("wait_ge", se.h, se.n)
        r.s.q("activation", out=nrow[:, 5:6], in_=nrow[:, 4:5], func=AF.Exp,
              scale=-0.5).then_inc(se.inc(), 1)
        r.v.q("wait_ge", se.h, se.n)
        r.v.q("tensor_tensor", out=nrow[:, 6:7], in0=nrow[:, 5:6],
              in1=nrow[:, 0:1], op=OP.mult).then_inc(ve.inc(), 1)
        r.v.q("wait_ge", ve.h, ve.n)
        r.v.q("tensor_scalar", out=nrow[:, 7:8], in0=nrow[:, 6:7],
              scalar1=-1.0, scalar2=None, op0=OP.mult).then_inc(ve.inc(), 1)
        r.v.q("wait_ge", ve.h, ve.n)
        r.v.q("tensor_scalar", out=dst_bf, in0=src_f32, scalar1=nrow[:, 5:6],
              scalar2=nrow[:, 7:8], op0=OP.mult, op1=OP.add)

    r.v.q("memset", ones_bf[:, :], 1.0)
    r.v.q("memset", ones_f32[:, :], 1.0)
    r.v.q("memset", onesr_bf[:, :], 1.0)
    r.v.q("memset", onesr_f32[:, :], 1.0)
    r.v.q("memset", mxacc[:, :], 0.0)
    r.y.q("dma_start", out=nonce_sb[:, :], in_=ap["nonce"]).then_inc(dX.inc(16), 16)
    r.y.q("dma_start", out=bia[:, :], in_=ap["bcols"]).then_inc(dX.inc(16), 16)
    r.g.q("dma_start", out=brow[0:1, 0:D], in_=ap["b2"]).then_inc(dW.inc(16), 16)
    r.g.q("dma_start", out=brow[0:1, D:2 * D], in_=ap["b_out"]).then_inc(dW.inc(16), 16)
    r.g.q("dma_start", out=buv_row[0:1, :], in_=ap["b_uv"]).then_inc(dW.inc(16), 16)
    r.v.q("wait_ge", dX.h, dX.n)
    r.v.q("tensor_scalar", out=bia_s[:, 0:32], in0=bia[:, CB["uq"]:CB["uq"] + 32],
          scalar1=SC, scalar2=None, op0=OP.mult)
    r.v.q("tensor_scalar", out=bia_s[:, 32:48], in0=bia[:, CB["rq"]:CB["rq"] + 16],
          scalar1=SC, scalar2=None, op0=OP.mult)
    r.v.q("tensor_copy", out=nonce_i[:, :], in_=nonce_sb[:, :]).then_inc(ve.inc(), 1)

    wstage_cm = nc.sbuf_tensor("wstage", [128, 2 * INTER], BF16)
    ws = wstage_cm.__enter__()

    def cache16(src, dst, rows, cols):
        for r0 in range(0, rows, 128):
            r.g.q("dma_start", out=ws[:, 0:cols],
                  in_=src[r0:r0 + 128, :]).then_inc(dW.inc(16), 16)
            r.g.q("wait_ge", dW.h, dW.n)
            r.g.q("dma_start", out=dst[r0:r0 + 128, :],
                  in_=ws[:, 0:cols]).then_inc(dW.inc(16), 16)
            r.g.q("wait_ge", dW.h, dW.n)

    cache16(ap["U1"], U1_16, D, RANK)
    cache16(ap["V1"], V1_16, RANK, 2 * INTER)
    cache16(ap["U2"], U2_16, INTER, RANK)
    cache16(ap["V2"], V2_16, RANK, D)
    r.g.q("wait_ge", dW.h, dW.n)
    join()
    wstage_cm.__exit__(None, None, None)

    _ffn_n = [0]

    def ffn_call(T, x_src_d, ln_src_d, out_dst_d, ffn1, dump_g0=False):
        _ffn_n[0] += 1
        sfx = f"_f{_ffn_n[0]}"
        ngr = T // G
        with ExitStack() as fx:
            fe = fx.enter_context
            LROT = 6
            lnrot = fe(nc.sbuf_tensor("lnrot" + sfx, [128, LROT * G], BF16))
            Pt = fe(nc.sbuf_tensor("Pt" + sfx, [128, NR * G], BF16))
            act = fe(nc.sbuf_tensor("act" + sfx, [128, NI * G], BF16))
            Zt = fe(nc.sbuf_tensor("Zt" + sfx, [128, NR * G], BF16))
            u1s = fe(nc.sbuf_tensor("u1s" + sfx, [128, 2 * RANK], BF16))
            v1s = fe(nc.sbuf_tensor("v1s" + sfx, [128, 4 * NR * 128], BF16))
            NIH = (NI + 3) // 4
            u2s = fe(nc.sbuf_tensor("u2s" + sfx, [128, 2 * NIH * 128], BF16))
            v2s = fe(nc.sbuf_tensor("v2s" + sfx, [128, 2 * 2048], BF16))
            xt = fe(nc.sbuf_tensor("xt" + sfx, [128, D], F32))
            ot = fe(nc.sbuf_tensor("ot" + sfx, [128, D], F32))
            silu_t = fe(nc.sbuf_tensor("silu_t" + sfx, [128, G], F32))
            sqt = xt
            lnt = fe(nc.sbuf_tensor("lnt_f" + sfx, [128, D], BF16))
            mrow = fe(nc.sbuf_tensor("mrow_f" + sfx, [128, 2], F32))
            nrow = fe(nc.sbuf_tensor("nrow_f" + sfx, [128, 8], F32))

            for gidx in range(ngr):
                def ln_req(k, g_=None):
                    g_ = gidx if g_ is None else g_
                    sl = k % LROT
                    r.y.q("dma_start",
                          out=lnrot[:, sl * G:(sl + 1) * G],
                          in_=ln_src_d[g_ * G:(g_ + 1) * G, k * 128:(k + 1) * 128],
                          transpose=True).then_inc(sLN[sl].inc(16), 16)

                pe_p0 = pe.n
                for k in range(min(LROT, ND)):
                    ln_req(k)
                for k in range(2):
                    r.g.q("dma_start", out=u1s[:, k * RANK:(k + 1) * RANK],
                          in_=U1_16[k * 128:(k + 1) * 128, :]).then_inc(sU1[k % 2].inc(16), 16)
                for k in range(ND):
                    r.t.q("wait_ge", sLN[k % LROT].h, sLN[k % LROT].n)
                    r.t.q("wait_ge", sU1[k % 2].h, sU1[k % 2].n)
                    for m in range(NR):
                        ps = PSA if m < 4 else PSB
                        mo = (m % 4) * G
                        sl = (k % 2) * RANK
                        MM(r.t, ps[:, mo:mo + G],
                           u1s[:, sl + m * 128:sl + (m + 1) * 128],
                           lnrot[:, (k % LROT) * G:(k % LROT + 1) * G],
                           start=(k == 0), stop=(k == ND - 1))
                    r.t.ops[-1].then_inc(pe.inc(), 1)
                    if k + LROT < ND:
                        r.y.q("wait_ge", pe.h, pe_p0 + k + 1)
                        ln_req(k + LROT)
                    if k + 2 < ND:
                        r.g.q("wait_ge", pe.h, pe_p0 + k + 1)
                        r.g.q("dma_start",
                              out=u1s[:, (k % 2) * RANK:(k % 2 + 1) * RANK],
                              in_=U1_16[(k + 2) * 128:(k + 3) * 128, :]).then_inc(sU1[k % 2].inc(16), 16)
                r.v.q("wait_ge", pe.h, pe.n)
                for m in range(NR):
                    ps = PSA if m < 4 else PSB
                    mo = (m % 4) * G
                    r.v.q("tensor_copy", out=Pt[:, m * G:(m + 1) * G],
                          in_=ps[:, mo:mo + G])
                r.v.ops[-1].then_inc(ve.inc(), 1)
                if dump_g0 and gidx == 0:
                    r.y.q("wait_ge", ve.h, ve.n)
                    r.y.q("dma_start", out=ffd["d_pt"], in_=Pt[:, :]).then_inc(dS.inc(16), 16)

                def v1_req(i):
                    sl = (i % 2) * 2 * NR * 128
                    o3, i3 = T3(v1s[:, sl:sl + NR * 128],
                                V1_16[:, i * 128:(i + 1) * 128], NR)
                    r.g.q("dma_start", out=o3, in_=i3).then_inc(sV1[i % 2].inc(16), 16)
                    o3, i3 = T3(v1s[:, sl + NR * 128:sl + 2 * NR * 128],
                                V1_16[:, INTER + i * 128:INTER + (i + 1) * 128], NR)
                    r.g.q("dma_start", out=o3, in_=i3).then_inc(sV1[i % 2].inc(16), 16)

                r.g.q("wait_ge", pe.h, pe.n)
                v1_req(0)
                v1_req(1)
                r.t.q("wait_ge", ve.h, ve.n)
                pe_a0, se_a0, ve_a0 = pe.n, se.n, ve.n
                for i in range(NI):
                    sl = (i % 2) * 2 * NR * 128
                    gp = PSA[:, (i % 2) * G:(i % 2 + 1) * G]
                    up = PSB[:, (i % 2) * G:(i % 2 + 1) * G]
                    r.t.q("wait_ge", sV1[i % 2].h, sV1[i % 2].n)
                    if i >= 2:
                        r.t.q("wait_ge", ve.h, ve_a0 + (i - 1))
                    for c in range(NR):
                        MM(r.t, gp, v1s[:, sl + c * 128:sl + (c + 1) * 128],
                           Pt[:, c * G:(c + 1) * G],
                           start=(c == 0), stop=(c == NR - 1))
                    for c in range(NR):
                        MM(r.t, up,
                           v1s[:, sl + NR * 128 + c * 128:sl + NR * 128 + (c + 1) * 128],
                           Pt[:, c * G:(c + 1) * G],
                           start=(c == 0), stop=(c == NR - 1))
                    r.t.ops[-1].then_inc(pe.inc(), 1)
                    if i + 2 < NI:
                        r.g.q("wait_ge", pe.h, pe_a0 + i + 1)
                        v1_req(i + 2)
                    r.s.q("wait_ge", pe.h, pe_a0 + i + 1)
                    if i >= 1:
                        r.s.q("wait_ge", ve.h, ve_a0 + i)
                    r.s.q("activation", out=silu_t[:, :], in_=gp, func=AF.Silu,
                          bias=bia[:, CB["b1g"] + i:CB["b1g"] + i + 1]).then_inc(se.inc(), 1)
                    r.v.q("wait_ge", se.h, se_a0 + i + 1)
                    r.v.q("scalar_tensor_tensor", out=act[:, i * G:(i + 1) * G],
                          in0=up, scalar=bia[:, CB["b1u"] + i:CB["b1u"] + i + 1],
                          in1=silu_t[:, :], op0=OP.add,
                          op1=OP.mult).then_inc(ve.inc(), 1)

                NQ4 = 4

                def u2_q_req(m, hh, slot):
                    o = slot * NIH * 128
                    lo, hi = hh * NIH * 128, min(NI * 128, (hh + 1) * NIH * 128)
                    o3, i3 = T3(u2s[:, o:o + (hi - lo)],
                                U2_16[lo:hi, m * 128:(m + 1) * 128], (hi - lo) // 128)
                    r.g.q("dma_start", out=o3, in_=i3).then_inc(sU2[slot].inc(16), 16)

                r.t.q("wait_ge", ve.h, ve.n)
                if dump_g0 and gidx == 0:
                    r.y.q("wait_ge", ve.h, ve.n)
                    r.y.q("dma_start", out=ffd["d_act"], in_=act[:, :]).then_inc(dS.inc(16), 16)
                r.g.q("wait_ge", pe.h, pe.n)
                u2_q_req(0, 0, 0)
                u2_q_req(0, 1, 1)
                ve_z0, pe_z0 = ve.n, pe.n
                part_done_pe = []
                hcount = 2
                for m in range(NR):
                    zp = PSA[:, (m % 2) * G:(m % 2 + 1) * G]
                    if m >= 2:
                        r.t.q("wait_ge", ve.h, ve_z0 + (m - 1))
                    for hh in range(NQ4):
                        idx = m * NQ4 + hh
                        slot = idx % 2
                        r.t.q("wait_ge", sU2[slot].h, sU2[slot].n)
                        lo, hi = hh * NIH, min(NI, (hh + 1) * NIH)
                        o = slot * NIH * 128
                        for i in range(lo, hi):
                            MM(r.t, zp, u2s[:, o + (i - lo) * 128:o + (i - lo + 1) * 128],
                               act[:, i * G:(i + 1) * G],
                               start=(i == 0), stop=(i == NI - 1))
                        r.t.ops[-1].then_inc(pe.inc(), 1)
                        part_done_pe.append(pe.n)
                        if hcount < NQ4 * NR:
                            nm, nh = divmod(hcount, NQ4)
                            r.g.q("wait_ge", pe.h, part_done_pe[hcount - 2])
                            u2_q_req(nm, nh, hcount % 2)
                            hcount += 1
                    r.v.q("wait_ge", pe.h, pe.n)
                    r.v.q("tensor_copy", out=Zt[:, m * G:(m + 1) * G],
                          in_=zp).then_inc(ve.inc(), 1)
                r.t.q("wait_ge", ve.h, ve.n)
                if dump_g0 and gidx == 0:
                    r.y.q("wait_ge", ve.h, ve.n)
                    r.y.q("dma_start", out=ffd["d_zt"], in_=Zt[:, :]).then_inc(dS.inc(16), 16)

                ntt = G // 128
                for tt in range(ntt):
                    row0 = gidx * G + tt * 128
                    r.y.q("wait_ge", ve.h, ve.n)
                    r.y.q("dma_start", out=xt[:, :],
                          in_=x_src_d[row0:row0 + 128, :]).then_inc(sXT.inc(16), 16)
                    for hf in range(2):
                        pp = PSA if hf == 0 else PSB
                        r.g.q("wait_ge", pe.h, pe.n)
                        for c in range(2):
                            r.g.q("dma_start",
                                  out=v2s[:, c * 2048:(c + 1) * 2048],
                                  in_=V2_16[c * 128:(c + 1) * 128,
                                            hf * 2048:(hf + 1) * 2048]).then_inc(sV2[c % 2].inc(16), 16)
                        if tt > 0 or hf > 0:
                            r.t.q("wait_ge", ve.h, ve.n)
                        pe_v0 = pe.n
                        for c in range(NR):
                            r.t.q("wait_ge", sV2[c % 2].h, sV2[c % 2].n)
                            MM(r.t, pp[:, 0:2048],
                               Zt[:, c * G + tt * 128:c * G + (tt + 1) * 128],
                               v2s[:, (c % 2) * 2048:(c % 2) * 2048 + 2048],
                               start=(c == 0), stop=False)
                            r.t.ops[-1].then_inc(pe.inc(), 1)
                            if c + 2 < NR:
                                r.g.q("wait_ge", pe.h, pe_v0 + c + 1)
                                r.g.q("dma_start",
                                      out=v2s[:, (c % 2) * 2048:(c % 2 + 1) * 2048],
                                      in_=V2_16[(c + 2) * 128:(c + 3) * 128,
                                                hf * 2048:(hf + 1) * 2048]).then_inc(sV2[c % 2].inc(16), 16)
                        MM(r.t, pp[:, 0:2048], onesr_bf[0:1, :],
                           brow[0:1, hf * 2048:(hf + 1) * 2048],
                           start=False, stop=True)
                        r.t.ops[-1].then_inc(pe.inc(), 1)
                        r.v.q("wait_ge", pe.h, pe.n)
                        r.v.q("wait_ge", sXT.h, sXT.n)
                        r.v.q("scalar_tensor_tensor",
                              out=ot[:, hf * 2048:(hf + 1) * 2048],
                              in0=pp[:, 0:2048], scalar=0.5,
                              in1=xt[:, hf * 2048:(hf + 1) * 2048],
                              op0=OP.mult, op1=OP.add).then_inc(ve.inc(), 1)
                    r.y.q("wait_ge", ve.h, ve.n)
                    if out_dst_d is not None:
                        r.y.q("dma_start", out=out_dst_d[row0:row0 + 128, :],
                              in_=ot[:, :]).then_inc(dO.inc(16), 16)
                    if ffn1:
                        if row0 < TQ:
                            r.y.q("dma_start", out=x1q_d[row0:row0 + 128, :],
                                  in_=ot[:, :]).then_inc(dO.inc(16), 16)
                        r.v.q("wait_ge", dO.h, dO.n)
                        ln_pass(ot[:, :], lnt[:, :], sqt, mrow, nrow)
                        r.v.ops[-1].then_inc(ve.inc(), 1)
                        r.y.q("wait_ge", ve.h, ve.n)
                        r.y.q("dma_start", out=ln2_d[row0:row0 + 128, :],
                              in_=lnt[:, :]).then_inc(dO.inc(16), 16)

    if n_phases >= 1:
        with ExitStack() as px:
            pf = px.enter_context
            xint = pf(nc.sbuf_tensor("xint", [128, D], F32))
            sqt1 = pf(nc.sbuf_tensor("sqt1", [128, D], F32))
            lnt1 = pf(nc.sbuf_tensor("lnt1", [128, D], BF16))
            mr1 = pf(nc.sbuf_tensor("mr1", [128, 2], F32))
            nr1 = pf(nc.sbuf_tensor("nr1", [128, 8], F32))
            for ti in range(TH // 128):
                r.y.q("wait_ge", ve.h, ve.n)
                r.y.q("dma_start", out=xint[:, :],
                      in_=ap["xh"][ti * 128:(ti + 1) * 128, :]).then_inc(dX.inc(16), 16)
                r.v.q("wait_ge", dX.h, dX.n)
                r.v.q("wait_ge", dO.h, dO.n)
                ln_pass(xint[:, :], lnt1[:, :], sqt1, mr1, nr1)
                r.v.ops[-1].then_inc(ve.inc(), 1)
                r.y.q("wait_ge", ve.h, ve.n)
                r.y.q("dma_start", out=ln1_d[ti * 128:(ti + 1) * 128, :],
                      in_=lnt1[:, :]).then_inc(dO.inc(16), 16)
            r.y.q("wait_ge", dO.h, dO.n)
        join()
        if n_phases >= 1.0:
            ffn_call(TH, ap["xh"], ln1_d, None, ffn1=True, dump_g0=dev_dump)
            r.y.q("wait_ge", dO.h, dO.n)
            join()

    CQ_cm = None
    if n_phases >= 1.6:
        def setup_regs(tag, TH_=TH):
            def fn(eng, regs):
                st = ExitStack()
                kr = st.enter_context(eng.register("koff_" + tag))
                pr = st.enter_context(eng.register("par_" + tag))
                eng.reg_mov(pr, eng.to_reg(eng.partition_id()))
                eng.reg_alu(pr, pr, 1, OP.bitwise_and)
                eng.reg_alu(kr, pr, TH_, OP.mult)
                regs["koff_" + tag] = eng.snap(kr, min_val=0, max_val=TH_)
                regs["par_" + tag] = eng.snap(pr, min_val=0, max_val=1)
                regs["_st_" + tag] = st
            return fn

        r.g.raw(setup_regs("g"))

        CKV_cm = nc.sbuf_tensor("CKV", [128, NCC * TH], BF16)
        CKV = CKV_cm.__enter__()
        CQ_cm = nc.sbuf_tensor("CQ", [128, NC1 * TQ], BF16)
        CQ = CQ_cm.__enter__()
        B2 = min(1024, TQ)

        with ExitStack() as px:
            pf = px.enter_context
            lnc = pf(nc.sbuf_tensor("lnc", [128, ND * B2], BF16))
            wsl = pf(nc.sbuf_tensor("wsl", [128, 2 * ND * 128], BF16))
            wrksl = pf(nc.sbuf_tensor("wrksl", [128, ND * 64], BF16))
            rkb = pf(nc.sbuf_tensor("rkb", [64, 1024], BF16))
            sqr_a = pf(nc.sbuf_tensor("sqr_a", [64, 1024], BF16))
            mrow2 = pf(nc.sbuf_tensor("mrow2", [1, 8], F32))

            WSRC = {"dq": ap["w_dq"], "dkv": ap["w_dkv"], "rk": ap["w_rk"]}

            def slab_req(j, kind, m):
                mw = 64 if kind == "rk" else 128
                dst = wrksl[:, 0:ND * 64] if kind == "rk" else \
                    wsl[:, (j % 2) * ND * 128:(j % 2) * ND * 128 + ND * 128]
                o3, i3 = T3(dst, WSRC[kind][:, m * mw:(m + 1) * mw], ND)
                sm = sRK if kind == "rk" else sWS[j % 2]
                r.g.q("dma_start", out=o3, in_=i3).then_inc(sm.inc(16), 16)

            for th4 in range(TH // B2):
                dT_0 = dT.n
                r.y.q("wait_ge", pe.h, pe.n)
                r.g.q("wait_ge", pe.h, pe.n)
                for k in range(ND):
                    r.y.q("dma_start", out=lnc[:, k * B2:(k + 1) * B2],
                          in_=ln2_d[th4 * B2:(th4 + 1) * B2, k * 128:(k + 1) * 128],
                          transpose=True).then_inc(dT.inc(16), 16)
                mlist = ([("dq", m) for m in range(NC1)] if th4 < TQ // B2 else [])
                mlist += [("dkv", m) for m in range(NCC)] + [("rk", 0)]
                for j in range(min(2, len(mlist))):
                    slab_req(j, *mlist[j])
                r.t.q("wait_ge", dT.h, dT_0 + 16 * ND)
                pe_0 = pe.n
                evac_ve = {}
                for j, (kind, m) in enumerate(mlist):
                    smw = sRK if kind == "rk" else sWS[j % 2]
                    r.t.q("wait_ge", smw.h, smw.n)
                    mw = 64 if kind == "rk" else 128
                    slab = wrksl[:, 0:ND * 64] if kind == "rk" else \
                        wsl[:, (j % 2) * ND * 128:(j % 2) * ND * 128 + ND * 128]
                    pp = PSB[0:64, 0:B2] if kind == "rk" else \
                        PSA[:, (j % 2) * B2:(j % 2) * B2 + B2]
                    if j >= 2 and kind != "rk":
                        r.t.q("wait_ge", ve.h, evac_ve[j - 2])
                    for k in range(ND):
                        MM(r.t, pp, slab[:, k * mw:(k + 1) * mw],
                           lnc[:, k * B2:(k + 1) * B2],
                           start=(k == 0), stop=(k == ND - 1))
                    r.t.ops[-1].then_inc(pe.inc(), 1)
                    if j + 2 < len(mlist):
                        r.g.q("wait_ge", pe.h, pe_0 + j + 1)
                        slab_req(j + 2, *mlist[j + 2])
                    r.v.q("wait_ge", pe.h, pe_0 + j + 1)
                    if kind == "dq":
                        r.v.q("tensor_scalar",
                              out=CQ[:, m * TQ + th4 * B2:m * TQ + (th4 + 1) * B2],
                              in0=pp, scalar1=bia[:, CB["dq"] + m:CB["dq"] + m + 1],
                              scalar2=None, op0=OP.add).then_inc(ve.inc(), 1)
                    elif kind == "dkv":
                        r.v.q("tensor_scalar",
                              out=CKV[:, m * TH + th4 * B2:m * TH + (th4 + 1) * B2],
                              in0=pp, scalar1=bia[:, CB["dkv"] + m:CB["dkv"] + m + 1],
                              scalar2=None, op0=OP.add).then_inc(ve.inc(), 1)
                    else:
                        r.v.q("wait_ge", dS.h, dS.n)
                        r.v.q("tensor_scalar", out=rkb[0:64, 0:B2], in0=pp,
                              scalar1=bia[0:64, CB["rk"]:CB["rk"] + 1],
                              scalar2=None, op0=OP.add).then_inc(ve.inc(), 1)
                        r.y.q("wait_ge", ve.h, ve.n)
                        r.y.q("dma_start",
                              out=Kr_loc.ap()[:, th4 * B2:(th4 + 1) * B2],
                              in_=rkb[0:64, 0:B2]).then_inc(dS.inc(16), 16)
                        r.s.q("wait_ge", ve.h, ve.n)
                        r.s.q("square", out=sqr_a[0:64, 0:B2],
                              in_=rkb[0:64, 0:B2]).then_inc(se.inc(), 1)
                        r.t.q("wait_ge", se.h, se.n)
                        if th4 >= 1:
                            r.t.q("wait_ge", ve.h, ve.n)
                        for jj in range(B2 // 512):
                            r.t.q("matmul",
                                  PSB[0:1, B2 + jj * 512:B2 + (jj + 1) * 512],
                                  lhsT=ones_bf[0:64, :],
                                  rhs=sqr_a[0:64, jj * 512:(jj + 1) * 512],
                                  start=True, stop=True)
                        r.t.ops[-1].then_inc(pe.inc(), 1)
                        r.v.q("wait_ge", pe.h, pe.n)
                        r.v.q("reduce_max", out=mrow2[0:1, th4:th4 + 1],
                              in_=PSB[0:1, B2:B2 + B2][0:1, 0:B2],
                              axis=AX.X).then_inc(ve.inc(), 1)
                    evac_ve[j] = ve.n
            r.v.q("wait_ge", ve.h, ve.n)
            r.v.q("reduce_max", out=mxacc[0:1, 32:33],
                  in_=mrow2[0:1, 0:TH // B2], axis=AX.X).then_inc(ve.inc(), 1)
            for c in range(NC1):
                r.y.q("wait_ge", ve.h, ve.n)
                r.y.q("dma_start", out=CQ_d[:, c * TQ:(c + 1) * TQ],
                      in_=CQ[:, c * TQ:(c + 1) * TQ]).then_inc(dS.inc(16), 16)
            r.y.q("wait_ge", dS.h, dS.n)
            join()
        CQ_cm.__exit__(None, None, None)

        if n_phases >= 1.7:
            with ExitStack() as px:
                pf = px.enter_context
                uksl = pf(nc.sbuf_tensor("uksl", [128, 2 * NCC * 128], BF16))
                ktile = pf(nc.sbuf_tensor("ktile", [128, 2 * 1024], BF16))
                sqk = pf(nc.sbuf_tensor("sqk", [128, 1024], BF16))
                mrow3 = pf(nc.sbuf_tensor("mrow3", [1, 8], F32))

                def uk_req(m):
                    o = (m % 2) * NCC * 128
                    o3, i3 = T3(uksl[:, o:o + NCC * 128],
                                ap["w_uk"][:, m * 128:(m + 1) * 128], NCC)
                    r.g.q("dma_start", out=o3, in_=i3).then_inc(sUK[m % 2].inc(16), 16)

                uk_req(0)
                uk_req(1)
                pe_0 = pe.n
                NKH = TH // 1024
                step = 0
                ve_ev = {}
                se_sq = {}
                for m in range(ND):
                    r.t.q("wait_ge", sUK[m % 2].h, sUK[m % 2].n)
                    o = (m % 2) * NCC * 128
                    if m + 2 < ND:
                        r.g.q("wait_ge", pe.h, pe_0 + 2 * (m + 1) * NKH)
                        uk_req(m + 2)
                    for kh in range(NKH):
                        pp = PSA[:, (step % 2) * 1024:(step % 2) * 1024 + 1024]
                        if step >= 2:
                            r.t.q("wait_ge", ve.h, ve_ev[step - 2])
                        for c in range(NCC):
                            MM(r.t, pp, uksl[:, o + c * 128:o + (c + 1) * 128],
                               CKV[:, c * TH + kh * 1024:c * TH + (kh + 1) * 1024],
                               start=(c == 0), stop=(c == NCC - 1))
                        r.t.ops[-1].then_inc(pe.inc(), 1)
                        r.v.q("wait_ge", pe.h, pe.n)
                        r.v.q("wait_ge", sKT[step % 2].h, sKT[step % 2].n)
                        r.v.q("tensor_scalar",
                              out=ktile[:, (step % 2) * 1024:(step % 2) * 1024 + 1024],
                              in0=pp, scalar1=bia[:, CB["uk"] + m:CB["uk"] + m + 1],
                              scalar2=None, op0=OP.add).then_inc(ve.inc(), 1)
                        ve_ev[step] = ve.n
                        r.y.q("wait_ge", ve.h, ve.n)
                        r.y.q("dma_start",
                              out=K_loc.ap()[m * 128:(m + 1) * 128,
                                             kh * 1024:(kh + 1) * 1024],
                              in_=ktile[:, (step % 2) * 1024:(step % 2) * 1024 + 1024]
                              ).then_inc(sKT[step % 2].inc(16), 16)
                        r.s.q("wait_ge", ve.h, ve_ev[step])
                        if step >= 1:
                            r.s.q("wait_ge", pe.h, pe.n)
                        r.s.q("square", out=sqk[:, :],
                              in_=ktile[:, (step % 2) * 1024:(step % 2) * 1024 + 1024]
                              ).then_inc(se.inc(), 1)
                        se_sq[step] = se.n
                        r.t.q("wait_ge", se.h, se_sq[step])
                        if step >= 1:
                            r.t.q("wait_ge", ve.h, ve.n)
                        for jj in range(2):
                            r.t.q("matmul", PSB[0:1, jj * 512:(jj + 1) * 512],
                                  lhsT=ones_bf[:, :],
                                  rhs=sqk[:, jj * 512:(jj + 1) * 512],
                                  start=True, stop=True)
                        r.t.ops[-1].then_inc(pe.inc(), 1)
                        r.v.q("wait_ge", pe.h, pe.n)
                        r.v.q("reduce_max", out=mrow3[0:1, kh:kh + 1],
                              in_=PSB[0:1, 0:1024], axis=AX.X).then_inc(ve.inc(), 1)
                        step += 1
                    r.v.q("wait_ge", ve.h, ve.n)
                    r.v.q("reduce_max", out=mxacc[0:1, m:m + 1],
                          in_=mrow3[0:1, 0:NKH], axis=AX.X).then_inc(ve.inc(), 1)
                r.y.q("wait_ge", ve.h, ve.n)
                r.y.q("wait_ge", dS.h, dS.n)
                r.y.q("wait_ge", sKT[0].h, sKT[0].n)
                r.y.q("wait_ge", sKT[1].h, sKT[1].n)
                join()

        if n_phases >= 1.8:
            with ExitStack() as px:
                pf = px.enter_context
                wuv4 = pf(nc.sbuf_tensor("wuv4", [128, NCC * D], BF16))
                vtile = pf(nc.sbuf_tensor("vtile", [128, D], BF16))
                dW_0 = dW.n
                for c in range(NCC):
                    r.g.q("dma_start", out=wuv4[:, c * D:(c + 1) * D],
                          in_=ap["w_uv"][c * 128:(c + 1) * 128, :]).then_inc(dW.inc(16), 16)
                r.t.q("wait_ge", dW.h, dW_0 + 16 * NCC)
                pe_0 = pe.n
                for kt in range(TH // 128):
                    if kt > 0:
                        r.t.q("wait_ge", ve.h, ve.n)
                    for c in range(NCC):
                        lh = CKV[:, c * TH + kt * 128:c * TH + (kt + 1) * 128]
                        MM(r.t, PSA[:, 0:2048], lh, wuv4[:, c * D:c * D + 2048],
                           start=(c == 0), stop=False)
                        MM(r.t, PSB[:, 0:2048], lh, wuv4[:, c * D + 2048:(c + 1) * D],
                           start=(c == 0), stop=False)
                    MM(r.t, PSA[:, 0:2048], onesr_bf[0:1, :], buv_row[0:1, 0:2048],
                       start=False, stop=True)
                    MM(r.t, PSB[:, 0:2048], onesr_bf[0:1, :], buv_row[0:1, 2048:D],
                       start=False, stop=True)
                    r.t.ops[-1].then_inc(pe.inc(), 1)
                    r.v.q("wait_ge", pe.h, pe_0 + kt + 1)
                    r.v.q("wait_ge", dS.h, dS.n)
                    r.v.q("tensor_copy", out=vtile[:, 0:2048], in_=PSA[:, 0:2048])
                    r.v.q("tensor_copy", out=vtile[:, 2048:D],
                          in_=PSB[:, 0:2048]).then_inc(ve.inc(), 1)
                    r.y.q("wait_ge", ve.h, ve.n)
                    r.y.q("dma_start",
                          out=V_loc.ap()[kt * 128:(kt + 1) * 128, :],
                          in_=vtile[:, :]).then_inc(dS.inc(16), 16)
                r.y.q("wait_ge", dS.h, dS.n)
                join()
            CKV_cm.__exit__(None, None, None)

        if n_phases >= 2:
            pub_ds0 = dS.n
            NKP, NVP = 4, 2
            npub = NKP + NVP + 2
            dS.inc(16 * npub)
            pub_tgt = dS.n
            bw_tgt = sems["bw"].n + 16
            sems["bw"].inc(16)
            dS_h, bw_h = dS.h, sems["bw"].h
            ve_h2, ve_pub = ve.h, ve.n

            def publish(eng, regs):
                eng.wait_ge(ve_h2, ve_pub)
                eng.wait_ge(dS_h, pub_ds0)
                for arm in nc.switch_core_id([eng], n=128):
                    par = (int(arm) % 8) & 1
                    for jj in range(NKP):
                        rr = D // NKP
                        eng.dma_start(
                            out=K_sh.ap()[jj * rr:(jj + 1) * rr,
                                          par * TH:(par + 1) * TH],
                            in_=K_loc.ap()[jj * rr:(jj + 1) * rr, :]).then_inc(dS_h, 16)
                    for jj in range(NVP):
                        rr = TH // NVP
                        eng.dma_start(
                            out=V_sh.ap()[par * TH + jj * rr:par * TH + (jj + 1) * rr, :],
                            in_=V_loc.ap()[jj * rr:(jj + 1) * rr, :]).then_inc(dS_h, 16)
                    eng.dma_start(out=Kr_sh.ap()[:, par * TH:(par + 1) * TH],
                                  in_=Kr_loc.ap()[:, :]).then_inc(dS_h, 16)
                    eng.dma_start(out=mx_sh.ap()[par:par + 1, :],
                                  in_=mxacc[0:1, :]).then_inc(dS_h, 16)
                    eng.wait_ge(dS_h, pub_tgt)
                    eng.dma_start(out=flags.ap()[0:1, par:par + 1],
                                  in_=nonce_sb[0:1, 0:1]).then_inc(bw_h, 16)
                eng.wait_ge(bw_h, bw_tgt)

            r.y.raw(publish)

            r.y.q("nop").then_inc(ph.inc(), 1)
            bar_tgt = ph.n
            ge_h = sems["bar"].h

            def barrier(eng, regs):
                eng.wait_ge(ph.h, bar_tgt)
                st = ExitStack()
                c = st.enter_context(eng.register("bar_c"))
                vv = st.enter_context(eng.register("bar_v"))
                kk = st.enter_context(eng.register("bar_k"))
                ps = st.enter_context(eng.register("bar_p"))
                eng.reg_load(kk, nonce_i[0:1, 0:1])
                eng.reg_mov(ps, regs["par_g"])
                eng.reg_alu(ps, ps, 1, OP.bitwise_xor)
                opar = eng.snap(ps, min_val=0, max_val=1)
                cnt = st.enter_context(eng.register("bar_n"))
                eng.reg_mov(c, 1)
                eng.reg_mov(cnt, 16)
                with eng.While(c):
                    eng.dma_start(out=flag_sb[0:1, 0:2],
                                  in_=flags.ap()[0:1, 0:2]).then_inc(ge_h, 16)
                    eng.wait_ge(ge_h, cnt)
                    eng.reg_alu(cnt, cnt, 16, OP.add)
                    eng.tensor_copy(out=flag_i[0:1, 0:2], in_=flag_sb[0:1, 0:2])
                    eng.reg_load(vv, flag_i[0:1, bass.ds(opar, 1)])
                    eng.reg_alu(c, vv, kk, OP.not_equal)
                st.close()

            r.g.raw(barrier)
            r.g.q("nop").then_inc(ph.inc(), 1)
            for eng in ALL:
                eng.q("wait_ge", ph.h, ph.n)

        if 1.7 <= n_phases < 1.8:
            CKV_cm.__exit__(None, None, None)
        if n_phases < 1.7:
            CKV_cm.__exit__(None, None, None)

    if n_phases >= 3:
        SEG = 2048
        NSEG = S // SEG
        KCS = SEG // 128
        QHW = TQ // 2
        with ExitStack() as px:
            pf = px.enter_context
            KRA = pf(nc.sbuf_tensor("KRA", [65, S], BF16))
            CQ = pf(nc.sbuf_tensor("CQ3", [128, NC1 * TQ], BF16))
            Kb = pf(nc.sbuf_tensor("Kb", [128, 2 * SEG], BF16))
            Vb = pf(nc.sbuf_tensor("Vb", [128, 2 * SEG], BF16))
            Qh = pf(nc.sbuf_tensor("Qh", [128, TQ], BF16))
            qra = pf(nc.sbuf_tensor("qra", [65, TQ], BF16))
            expS = pf(nc.sbuf_tensor("expS", [128, 2 * 1024], BF16))
            accq = pf(nc.sbuf_tensor("accq", [128, TQ], F32))
            invb = pf(nc.sbuf_tensor("invb", [128, TQ], F32))
            Osb = pf(nc.sbuf_tensor("Osb", [128, TQ], BF16))
            wuq2 = pf(nc.sbuf_tensor("wuq2", [128, 2 * NC1 * 128], BF16))
            wrq2 = pf(nc.sbuf_tensor("wrq2", [128, 2 * NC1 * 64], BF16))
            sq32 = pf(nc.sbuf_tensor("sq32", [128, 512], F32))
            sqr2 = pf(nc.sbuf_tensor("sqr2", [64, 512], F32))
            qn2r = pf(nc.sbuf_tensor("qn2r", [1, TQ], F32))
            srow = pf(nc.sbuf_tensor("srow", [1, TQ], F32))
            mxr0 = pf(nc.sbuf_tensor("mxr0", [1, 64], F32))
            mxr1 = pf(nc.sbuf_tensor("mxr1", [1, 64], F32))
            lkr = pf(nc.sbuf_tensor("lkr", [1, 192], F32))
            srow2 = pf(nc.sbuf_tensor("srow2", [1, TQ], F32))

            r.y.q("dma_start", out=KRA[0:64, :],
                  in_=Kr_sh.ap()[:, :]).then_inc(dX.inc(16), 16)
            r.v.q("memset", KRA[64:65, :], 1.0).then_inc(ve.inc(), 1)
            for c in range(NC1):
                r.y.q("dma_start", out=CQ[:, c * TQ:(c + 1) * TQ],
                      in_=CQ_d[:, c * TQ:(c + 1) * TQ]).then_inc(dX.inc(16), 16)
            r.y.q("dma_start", out=mxr0[0:1, :],
                  in_=mx_sh.ap()[0:1, :]).then_inc(dX.inc(16), 16)
            r.y.q("dma_start", out=mxr1[0:1, :],
                  in_=mx_sh.ap()[1:2, :]).then_inc(dX.inc(16), 16)
            pre_dX, pre_ve = dX.n, ve.n
            r.v.q("wait_ge", dX.h, pre_dX)
            r.v.q("tensor_tensor", out=lkr[0:1, 0:64], in0=mxr0[0:1, :],
                  in1=mxr1[0:1, :], op=OP.max).then_inc(ve.inc(), 1)
            r.v.q("wait_ge", ve.h, ve.n)
            r.v.q("tensor_scalar", out=lkr[0:1, 64:96], in0=lkr[0:1, 0:32],
                  scalar1=lkr[0:1, 32:33], scalar2=1e-30,
                  op0=OP.add, op1=OP.add).then_inc(ve.inc(), 1)
            r.s.q("wait_ge", ve.h, ve.n)
            r.s.q("activation", out=lkr[0:1, 96:128], in_=lkr[0:1, 64:96],
                  func=AF.Ln).then_inc(se.inc(), 1)
            r.v.q("wait_ge", se.h, se.n)
            r.v.q("tensor_scalar", out=lkr[0:1, 128:160], in0=lkr[0:1, 96:128],
                  scalar1=0.5, scalar2=None, op0=OP.mult).then_inc(ve.inc(), 1)
            pre_ve = ve.n

            def pf_head(h, wait_pe=None):
                if wait_pe is not None:
                    r.g.q("wait_ge", pe.h, wait_pe)
                o3, i3 = T3(wuq2[:, (h % 2) * NC1 * 128:(h % 2 + 1) * NC1 * 128],
                            ap["w_uq"][:, h * 128:(h + 1) * 128], NC1)
                r.g.q("dma_start", out=o3, in_=i3).then_inc(sWQ[h % 2].inc(16), 16)
                o3, i3 = T3(wrq2[:, (h % 2) * NC1 * 64:(h % 2 + 1) * NC1 * 64],
                            ap["w_rq"][:, h * 64:(h + 1) * 64], NC1)
                r.g.q("dma_start", out=o3, in_=i3).then_inc(sWQ[h % 2].inc(16), 16)
                return sWQ[h % 2].n

            def pf_seg(h, seg, wait_pe=None):
                sl = (seg % 2) * SEG
                if wait_pe is not None:
                    r.y.q("wait_ge", pe.h, wait_pe)
                r.y.q("dma_start", out=Kb[:, sl:sl + SEG],
                      in_=K_sh.ap()[h * 128:(h + 1) * 128,
                                    seg * SEG:(seg + 1) * SEG]).then_inc(dX.inc(16), 16)
                o3, i3 = T3(Vb[:, sl:sl + SEG],
                            V_sh.ap()[seg * SEG:(seg + 1) * SEG,
                                      h * 128:(h + 1) * 128], SEG // 128)
                r.y.q("dma_start", out=o3, in_=i3).then_inc(dX.inc(16), 16)
                return dX.n

            pfw = {0: pf_head(0)}
            seg_dX = {(0, 0): pf_seg(0, 0)}
            evacC_ve = {-1: pre_ve}
            for h in range(H):
                slq = (h % 2) * NC1 * 128
                slr = (h % 2) * NC1 * 64
                r.t.q("wait_ge", sWQ[h % 2].h, pfw[h])
                r.t.q("wait_ge", dX.h, pre_dX)
                r.t.q("wait_ge", ve.h, max(evacC_ve[h - 1], pre_ve))
                for c in range(NC1):
                    MM(r.t, PSA[:, 0:TQ], wuq2[:, slq + c * 128:slq + (c + 1) * 128],
                       CQ[:, c * TQ:(c + 1) * TQ],
                       start=(c == 0), stop=(c == NC1 - 1))
                r.t.ops[-1].then_inc(pe.inc(), 1)
                if h + 1 < H:
                    pfw[h + 1] = pf_head(h + 1, wait_pe=(None if h == 0 else pe.n))
                r.v.q("wait_ge", pe.h, pe.n)
                r.v.q("tensor_scalar", out=Qh[:, :], in0=PSA[:, 0:TQ],
                      scalar1=SC, scalar2=bia_s[:, h:h + 1],
                      op0=OP.mult, op1=OP.add).then_inc(ve.inc(), 1)
                for c in range(NC1):
                    MM(r.t, PSB[0:64, 0:TQ], wrq2[:, slr + c * 64:slr + (c + 1) * 64],
                       CQ[:, c * TQ:(c + 1) * TQ],
                       start=(c == 0), stop=(c == NC1 - 1))
                r.t.ops[-1].then_inc(pe.inc(), 1)
                r.v.q("wait_ge", pe.h, pe.n)
                r.v.q("tensor_scalar", out=qra[0:64, :], in0=PSB[0:64, 0:TQ],
                      scalar1=SC,
                      scalar2=bia_s[64 * (h % 2):64 * (h % 2) + 64,
                                    32 + h // 2:33 + h // 2],
                      op0=OP.mult, op1=OP.add)
                r.v.q("memset", accq[:, :], 0.0).then_inc(ve.inc(), 1)
                ve_qr = ve.n
                pe_n0 = pe.n
                po_n0 = po_s.n
                ve_n0 = ve.n
                NQC = TQ // 512
                for j in range(NQC):
                    r.g.q("wait_ge", ve.h, ve_qr)
                    if j >= 1:
                        r.g.q("wait_ge", pe.h, pe_n0 + j)
                    r.g.q("tensor_tensor", out=sq32[:, :],
                          in0=Qh[:, j * 512:(j + 1) * 512],
                          in1=Qh[:, j * 512:(j + 1) * 512], op=OP.mult)
                    r.g.q("tensor_tensor", out=sqr2[0:64, :],
                          in0=qra[0:64, j * 512:(j + 1) * 512],
                          in1=qra[0:64, j * 512:(j + 1) * 512],
                          op=OP.mult).then_inc(po_s.inc(), 1)
                    r.t.q("wait_ge", po_s.h, po_s.n)
                    if j >= 2:
                        r.t.q("wait_ge", ve.h, ve_n0 + j - 1)
                    pq = PSA[0:1, (j % 2) * 512:(j % 2) * 512 + 512]
                    r.t.q("matmul", pq, lhsT=ones_f32[:, :], rhs=sq32[:, :],
                          start=True, stop=False)
                    r.t.q("matmul", pq, lhsT=ones_f32[0:64, :], rhs=sqr2[0:64, :],
                          start=False, stop=True).then_inc(pe.inc(), 1)
                    r.v.q("wait_ge", pe.h, pe.n)
                    r.v.q("tensor_scalar", out=qn2r[0:1, j * 512:(j + 1) * 512],
                          in0=pq, scalar1=1e-30, scalar2=None,
                          op0=OP.add).then_inc(ve.inc(), 1)
                r.s.q("wait_ge", ve.h, ve.n)
                r.s.q("activation", out=srow[0:1, :], in_=qn2r[0:1, :],
                      func=AF.Ln).then_inc(se.inc(), 1)
                r.v.q("wait_ge", se.h, se.n)
                r.v.q("tensor_scalar", out=srow2[0:1, :], in0=srow[0:1, :],
                      scalar1=0.5, scalar2=lkr[0:1, 128 + h:129 + h],
                      op0=OP.mult, op1=OP.add).then_inc(ve.inc(), 1)
                r.s.q("wait_ge", ve.h, ve.n)
                r.s.q("activation", out=srow[0:1, :], in_=srow2[0:1, :],
                      func=AF.Exp).then_inc(se.inc(), 1)
                r.v.q("wait_ge", se.h, se.n)
                r.v.q("tensor_scalar", out=qra[64:65, :], in0=srow[0:1, :],
                      scalar1=-1.0, scalar2=None, op0=OP.mult).then_inc(ve.inc(), 1)
                ve_mrow = ve.n

                r.t.q("wait_ge", ve.h, ve_mrow)
                scnt = 0
                pe_sc = {}
                pe_av = {}
                se_ex = {}
                ve_sm = {}
                order = []
                for seg in range(NSEG):
                    for qh in range(2):
                        qo = qh * QHW
                        r.t.q("wait_ge", dX.h, seg_dX[(h, seg)])
                        if qh == 1 and NSEG > 1:
                            nh, nseg = (h, seg + 1) if seg + 1 < NSEG else (h + 1, 0)
                            if nh < H and (nh, nseg) not in seg_dX:
                                seg_dX[(nh, nseg)] = pf_seg(nh, nseg, wait_pe=pe.n)
                        for kc in range(KCS):
                            gk = seg * KCS + kc
                            idx = (seg, qh, kc)
                            prev2 = order[-2] if len(order) >= 2 else None
                            prev1 = order[-1] if len(order) >= 1 else None
                            if prev2 is not None:
                                r.t.q("wait_ge", se.h, se_ex[prev2])
                            sps = PSA[:, (scnt % 2) * 1024:(scnt % 2) * 1024 + 1024]
                            MM(r.t, sps[:, 0:QHW],
                               Kb[:, (seg % 2) * SEG + kc * 128:(seg % 2) * SEG + (kc + 1) * 128],
                               Qh[:, qo:qo + QHW], start=True, stop=False)
                            MM(r.t, sps[:, 0:QHW],
                               KRA[:, gk * 128:(gk + 1) * 128],
                               qra[:, qo:qo + QHW], start=False, stop=True)
                            r.t.ops[-1].then_inc(pe.inc(), 1)
                            pe_sc[idx] = pe.n
                            if prev1 is not None:
                                pseg, pqh, pkc = prev1
                                r.t.q("wait_ge", se.h, se_ex[prev1])
                                pgk = pseg * KCS + pkc
                                MM(r.t, PSB[:, pqh * QHW:(pqh + 1) * QHW],
                                   Vb[:, (pseg % 2) * SEG + pkc * 128:(pseg % 2) * SEG + (pkc + 1) * 128],
                                   expS[:, ((scnt - 1) % 2) * 1024:((scnt - 1) % 2) * 1024 + QHW],
                                   start=(pgk == 0), stop=(pgk == NKC - 1))
                                r.t.ops[-1].then_inc(pe.inc(), 1)
                                pe_av[prev1] = pe.n
                            r.s.q("wait_ge", pe.h, pe_sc[idx])
                            if prev2 is not None:
                                r.s.q("wait_ge", pe.h, pe_av[prev2])
                                r.s.q("wait_ge", ve.h, ve_sm[prev2])
                            r.s.q("activation",
                                  out=expS[:, (scnt % 2) * 1024:(scnt % 2) * 1024 + QHW],
                                  in_=sps[:, 0:QHW], func=AF.Exp).then_inc(se.inc(), 1)
                            se_ex[idx] = se.n
                            r.v.q("wait_ge", se.h, se_ex[idx])
                            r.v.q("tensor_tensor", out=accq[:, qo:qo + QHW],
                                  in0=accq[:, qo:qo + QHW],
                                  in1=expS[:, (scnt % 2) * 1024:(scnt % 2) * 1024 + QHW],
                                  op=OP.add).then_inc(ve.inc(), 1)
                            ve_sm[idx] = ve.n
                            order.append(idx)
                            scnt += 1
                prev1 = order[-1]
                pseg, pqh, pkc = prev1
                r.t.q("wait_ge", se.h, se_ex[prev1])
                pgk = pseg * KCS + pkc
                MM(r.t, PSB[:, pqh * QHW:(pqh + 1) * QHW],
                   Vb[:, (pseg % 2) * SEG + pkc * 128:(pseg % 2) * SEG + (pkc + 1) * 128],
                   expS[:, ((scnt - 1) % 2) * 1024:((scnt - 1) % 2) * 1024 + QHW],
                   start=(pgk == 0), stop=True)
                r.t.ops[-1].then_inc(pe.inc(), 1)
                pe_av[prev1] = pe.n
                if h + 1 < H and (h + 1, 0) not in seg_dX:
                    seg_dX[(h + 1, 0)] = pf_seg(h + 1, 0, wait_pe=pe.n)

                r.t.q("wait_ge", ve.h, ve.n)
                for j in range(TQ // 512):
                    r.t.q("matmul", PSA[0:1, j * 512:(j + 1) * 512],
                          lhsT=ones_f32[:, :], rhs=accq[:, j * 512:(j + 1) * 512],
                          start=True, stop=True)
                r.t.ops[-1].then_inc(pe.inc(), 1)
                r.v.q("wait_ge", pe.h, pe.n)
                r.v.q("tensor_copy", out=srow2[0:1, :],
                      in_=PSA[0:1, 0:TQ]).then_inc(ve.inc(), 1)
                r.v.q("wait_ge", ve.h, ve.n)
                r.v.q("reciprocal", out=srow[0:1, :],
                      in_=srow2[0:1, :]).then_inc(ve.inc(), 1)
                r.t.q("wait_ge", ve.h, ve.n)
                for j in range(TQ // 512):
                    r.t.q("matmul", PSA[:, j * 512:(j + 1) * 512],
                          lhsT=onesr_f32[0:1, :],
                          rhs=srow[0:1, j * 512:(j + 1) * 512],
                          start=True, stop=True)
                r.t.ops[-1].then_inc(pe.inc(), 1)
                r.v.q("wait_ge", pe.h, pe.n)
                r.v.q("wait_ge", dS.h, dS.n)
                r.v.q("tensor_copy", out=invb[:, :],
                      in_=PSA[:, 0:TQ]).then_inc(ve.inc(), 1)
                r.v.q("wait_ge", ve.h, ve.n)
                r.v.q("tensor_tensor", out=Osb[:, :], in0=PSB[:, 0:TQ],
                      in1=invb[:, :], op=OP.mult).then_inc(ve.inc(), 1)
                evacC_ve[h] = ve.n
                r.y.q("wait_ge", ve.h, ve.n)
                r.y.q("dma_start", out=O_d[h * 128:(h + 1) * 128, :],
                      in_=Osb[:, :]).then_inc(dS.inc(16), 16)
            r.y.q("wait_ge", dS.h, dS.n)
        join()

    if n_phases >= 4:
        with ExitStack() as px:
            pf = px.enter_context
            Oh = pf(nc.sbuf_tensor("Oh", [128, ND * (TQ // 2)], BF16))
            wosl = pf(nc.sbuf_tensor("wosl", [128, 2 * ND * 512], BF16))
            piece = pf(nc.sbuf_tensor("piece", [128, 2 * 512], F32))
            THW = TQ // 2
            for th in range(2):
                dX_0 = dX.n
                r.y.q("wait_ge", pe.h, pe.n)
                for cq in range(4):
                    o3, i3 = T3(Oh[:, cq * (ND // 4) * THW:(cq + 1) * (ND // 4) * THW],
                                O_d.ap()[cq * (D // 4):(cq + 1) * (D // 4),
                                         th * THW:(th + 1) * THW], ND // 4)
                    r.y.q("dma_start", out=o3, in_=i3).then_inc(dX.inc(16), 16)
                r.g.q("wait_ge", pe.h, pe.n)
                o3, i3 = T3(wosl[:, 0:ND * 512], ap["w_out"][:, 0:512], ND)
                r.g.q("dma_start", out=o3, in_=i3).then_inc(sWO[0].inc(16), 16)
                r.t.q("wait_ge", dX.h, dX_0 + 64)
                pe_0 = pe.n
                step = 0
                for dc in range(ND // 4):
                    if dc + 1 < ND // 4:
                        r.g.q("wait_ge", pe.h, pe_0 + step)
                        o3, i3 = T3(wosl[:, ((dc + 1) % 2) * ND * 512:((dc + 1) % 2 + 1) * ND * 512],
                                    ap["w_out"][:, (dc + 1) * 512:(dc + 2) * 512], ND)
                        r.g.q("dma_start", out=o3, in_=i3).then_inc(sWO[(dc + 1) % 2].inc(16), 16)
                    r.t.q("wait_ge", sWO[dc % 2].h, sWO[dc % 2].n)
                    osl = (dc % 2) * ND * 512
                    for tt in range(THW // 128):
                        pp = PSA[:, (tt % 2) * 512:(tt % 2) * 512 + 512]
                        if step >= 2:
                            r.t.q("wait_ge", ve.h, ve.n - 1)
                        for hc in range(ND):
                            r.t.q("matmul", pp,
                                  lhsT=Oh[:, hc * THW + tt * 128:hc * THW + (tt + 1) * 128],
                                  rhs=wosl[:, osl + hc * 512:osl + (hc + 1) * 512],
                                  start=(hc == 0), stop=False)
                        r.t.q("matmul", pp, lhsT=onesr_bf[0:1, :],
                              rhs=brow[0:1, D + dc * 512:D + (dc + 1) * 512],
                              start=False, stop=True).then_inc(pe.inc(), 1)
                        r.v.q("wait_ge", pe.h, pe.n)
                        r.v.q("wait_ge", sPC[tt % 2].h, sPC[tt % 2].n)
                        r.v.q("tensor_copy", out=piece[:, (tt % 2) * 512:(tt % 2) * 512 + 512],
                              in_=pp).then_inc(ve.inc(), 1)
                        r.y.q("wait_ge", ve.h, ve.n)
                        r.y.q("dma_start",
                              out=x2r_d[th * THW + tt * 128:th * THW + (tt + 1) * 128,
                                        dc * 512:(dc + 1) * 512],
                              in_=piece[:, (tt % 2) * 512:(tt % 2) * 512 + 512]).then_inc(sPC[tt % 2].inc(16), 16)
                        step += 1
            r.y.q("wait_ge", sPC[0].h, sPC[0].n)
            r.y.q("wait_ge", sPC[1].h, sPC[1].n)
            r.y.q("wait_ge", dS.h, dS.n)
        join()
        with ExitStack() as px:
            pf = px.enter_context
            xa = pf(nc.sbuf_tensor("xa", [128, D], F32))
            xb = pf(nc.sbuf_tensor("xb", [128, D], F32))
            xc = pf(nc.sbuf_tensor("xc", [128, D], F32))
            sq4 = pf(nc.sbuf_tensor("sq4", [128, D], F32))
            lnt4 = pf(nc.sbuf_tensor("lnt4", [128, D], BF16))
            mr4 = pf(nc.sbuf_tensor("mr4", [128, 2], F32))
            nr4 = pf(nc.sbuf_tensor("nr4", [128, 8], F32))
            for ti in range(TQ // 128):
                r.y.q("wait_ge", ve.h, ve.n)
                r.y.q("dma_start", out=xa[:, :],
                      in_=x2r_d[ti * 128:(ti + 1) * 128, :]).then_inc(dX.inc(16), 16)
                r.y.q("dma_start", out=xb[:, :],
                      in_=x1q_d[ti * 128:(ti + 1) * 128, :]).then_inc(dX.inc(16), 16)
                r.v.q("wait_ge", dX.h, dX.n)
                r.v.q("tensor_tensor", out=xc[:, :], in0=xa[:, :], in1=xb[:, :],
                      op=OP.add).then_inc(ve.inc(), 1)
                r.y.q("wait_ge", ve.h, ve.n)
                r.y.q("dma_start", out=x2_d[ti * 128:(ti + 1) * 128, :],
                      in_=xc[:, :]).then_inc(dO.inc(16), 16)
                r.v.q("wait_ge", dO.h, dO.n)
                ln_pass(xc[:, :], lnt4[:, :], sq4, mr4, nr4)
                r.v.ops[-1].then_inc(ve.inc(), 1)
                r.y.q("wait_ge", ve.h, ve.n)
                r.y.q("dma_start", out=ln3_d[ti * 128:(ti + 1) * 128, :],
                      in_=lnt4[:, :]).then_inc(dO.inc(16), 16)
            r.y.q("wait_ge", dO.h, dO.n)
        join()

    if n_phases >= 5:
        ffn_call(TQ, x2_d, ln3_d, out_d, ffn1=False)
        r.y.q("wait_ge", dO.h, dO.n)
        join()

    if dev_dump:
        for name, (dst, src, shp) in dbg.items():
            r.g.q("dma_start", out=dst, in_=src.ap()[:, :]).then_inc(dW.inc(16), 16)
        r.g.q("wait_ge", dW.h, dW.n)

    r.y.q("wait_ge", dO.h, dO.n)

    ctx.close()

    with nc.Block() as block:
        @block.tensor
        def _(t):
            r.t.replay(t, REGS)

        @block.vector
        def _(v):
            r.v.replay(v, REGS)

        @block.scalar
        def _(s):
            r.s.replay(s, REGS)

        @block.gpsimd
        def _(g):
            r.g.replay(g, REGS)

        @block.sync
        def _(y):
            r.y.replay(y, REGS)

    nc.compile()
    return nc, dbg


_NAME = {
    "w_dq": "L0_attn_w_dq", "w_uq": "L0_attn_w_uq", "w_dkv": "L0_attn_w_dkv",
    "w_uk": "L0_attn_w_uk", "w_uv": "L0_attn_w_uv", "w_rq": "L0_attn_w_rq",
    "w_rk": "L0_attn_w_rk", "w_out": "L0_attn_w_out",
    "b_uv": "L0_attn_b_uv", "b_out": "L0_attn_b_out", "b2": "L0_ffn_b2",
    "U1": "L0_ffn_U1", "V1": "L0_ffn_V1", "U2": "L0_ffn_U2", "V2": "L0_ffn_V2",
}


def _bcols_from(inputs):
    bc = np.zeros((128, 269), np.float32)
    off = 0

    def put(vec, n, rows=128):
        nonlocal off
        v = np.asarray(vec, np.float32).reshape(n, rows)
        bc[0:rows, off:off + n] = v.T
        off += n

    put(inputs["L0_attn_b_dq"], 12)
    put(inputs["L0_attn_b_uq"], 32)
    put(inputs["L0_attn_b_dkv"], 4)
    put(inputs["L0_attn_b_uk"], 32)
    put(inputs["L0_attn_b_rq"], 16)
    v = np.asarray(inputs["L0_attn_b_rk"], np.float32).reshape(64)
    bc[0:64, off] = v
    off += 1
    b1 = np.asarray(inputs["L0_ffn_b1"], np.float32)
    put(b1[:INTER], 86)
    put(b1[INTER:], 86)
    assert off == 269
    return bc


def make_inmaps(inputs, TH, rng=None):
    S, TQ = 2 * TH, TH // 2
    x = np.ascontiguousarray(np.asarray(inputs["L0_attn_sequence"], np.float32))
    bc = _bcols_from(inputs)
    if rng is None:
        rng = np.random.default_rng()
    nonce = rng.integers(1, 2 ** 30, size=(1, 4)).astype(np.float32)
    common = {"bcols": bc, "nonce": nonce}
    for k, src in _NAME.items():
        a = np.asarray(inputs[src], np.float32)
        if a.ndim == 1:
            a = a.reshape(1, -1)
        common[k] = np.ascontiguousarray(a)
    in_maps = []
    for c in range(8):
        b, w = c // 4, c % 4
        xb = x[b, :S]
        if w == 0:
            xh = xb[0:TH]
        elif w == 1:
            xh = xb[TH:2 * TH]
        elif w == 2:
            xh = np.concatenate([xb[TQ:TH], xb[0:TQ]], 0)
        else:
            xh = np.concatenate([xb[TH + TQ:2 * TH], xb[TH:TH + TQ]], 0)
        in_maps.append({**common, "xh": np.ascontiguousarray(xh)})
    return in_maps


def assemble_out(results, TH, full_shape):
    S, TQ = 2 * TH, TH // 2
    out = np.zeros((2, S, D), np.float32)
    qr = {0: (0, TQ), 1: (TH, TH + TQ), 2: (TQ, TH), 3: (TH + TQ, 2 * TH)}
    for c in range(8):
        b, w = c // 4, c % 4
        lo, hi = qr[w]
        out[b, lo:hi] = np.asarray(results[c]["out"], np.float32)
    return out


def kernel(**inputs):
    TH = 4096
    nc, _ = build_program(TH, n_phases=99, dev_dump=False)
    in_maps = make_inmaps(inputs, TH)
    res = run_bass_kernel_spmd(nc, in_maps, core_ids=list(range(8)))
    return assemble_out(res.results, TH, None)


# version 05
# speedup vs baseline: 1.0132x; 1.0132x over previous
import math
import numpy as np
import ml_dtypes
from contextlib import ExitStack

import concourse.bass as bass
import concourse.bacc as bacc_mod
import concourse.mybir as mybir
from concourse.bass_utils import run_bass_kernel_spmd

F32 = mybir.dt.float32
BF16 = mybir.dt.bfloat16
I32 = mybir.dt.int32
AF = mybir.ActivationFunctionType
OP = mybir.AluOpType
AX = mybir.AxisListType

D, DC1, DC, DR, H, DH = 4096, 1536, 512, 64, 32, 128
INTER, RANK = 11008, 1024
ND, NR, NI, NC1, NCC = D // 128, RANK // 128, INTER // 128, DC1 // 128, DC // 128
SC = 1.0 / math.sqrt(DH + DR)
G = 512


class _Inst:
    __slots__ = ("meth", "args", "kwargs", "incs")

    def __init__(self, meth, args, kwargs):
        self.meth, self.args, self.kwargs = meth, args, kwargs
        self.incs = []

    def then_inc(self, sem, n):
        self.incs.append((sem, n))
        return self


def _rs(a):
    return a() if callable(a) else a


class EngRec:
    def __init__(self, name):
        self.name = name
        self.ops = []

    def q(self, meth, *args, **kwargs):
        op = _Inst(meth, args, kwargs)
        self.ops.append(op)
        return op

    def raw(self, fn):
        self.ops.append(_Inst("__raw__", (fn,), {}))

    def replay(self, eng, regs):
        for op in self.ops:
            if op.meth == "__raw__":
                op.args[0](eng, regs)
                continue
            args = [_rs(a) for a in op.args]
            kwargs = {k: _rs(v) for k, v in op.kwargs.items()}
            inst = getattr(eng, op.meth)(*args, **kwargs)
            for sem, n in op.incs:
                inst.then_inc(sem, n)


class Sem:
    def __init__(self, nc, ctx, name):
        self.h = ctx.enter_context(nc.semaphore(name))
        self.n = 0

    def inc(self, k=1):
        self.n += k
        return self.h


def T3(dst2d, src2d, c):
    f = src2d.shape[-1]
    return (dst2d.rearrange("p (c f) -> p c f", c=c),
            src2d.rearrange("(c p) f -> p c f", p=128))


def MM(t, out_ap, lhsT_ap, rhs_ap, start, stop):
    N = out_ap.shape[-1]
    for o in range(0, N, 512):
        e_ = min(N, o + 512)
        t.q("matmul", out_ap[:, o:e_], lhsT=lhsT_ap, rhs=rhs_ap[:, o:e_],
            start=start, stop=stop)


def build_program(TH, n_phases=99, dev_dump=False):
    S, TQ = 2 * TH, TH // 2
    NKC = S // 128
    nc = bacc_mod.Bacc("TRN2", target_bir_lowering=False)
    REGS = {}

    ap = {}
    ap["xh"] = nc.dram_tensor("xh", [TH, D], F32, kind="ExternalInput").ap()
    ap["nonce"] = nc.dram_tensor("nonce", [1, 4], F32, kind="ExternalInput").ap()
    ap["bcols"] = nc.dram_tensor("bcols", [128, 269], F32, kind="ExternalInput").ap()
    for n, shp in [
        ("w_dq", [D, DC1]), ("w_uq", [DC1, D]), ("w_dkv", [D, DC]),
        ("w_uk", [DC, D]), ("w_uv", [DC, D]), ("w_rq", [DC1, H * DR]),
        ("w_rk", [D, DR]), ("w_out", [D, D]), ("b_uv", [1, D]),
        ("b_out", [1, D]), ("b2", [1, D]),
        ("U1", [D, RANK]), ("V1", [RANK, 2 * INTER]),
        ("U2", [INTER, RANK]), ("V2", [RANK, D]),
    ]:
        ap[n] = nc.dram_tensor(n, shp, F32, kind="ExternalInput").ap()
    out_d = nc.dram_tensor("out", [TQ, D], F32, kind="ExternalOutput").ap()

    K_sh = nc.dram_tensor("K_sh", [D, S], BF16, addr_space="Shared")
    V_sh = nc.dram_tensor("V_sh", [S, D], BF16, addr_space="Shared")
    Kr_sh = nc.dram_tensor("Kr_sh", [DR, S], BF16, addr_space="Shared")
    flags = nc.dram_tensor("flags", [1, 2], F32, addr_space="Shared")
    mx_sh = nc.dram_tensor("mx_sh", [2, 64], F32, addr_space="Shared")
    ln1_d = nc.dram_tensor("ln1_d", [TH, D], BF16)
    ln2_d = nc.dram_tensor("ln2_d", [TH, D], BF16)
    ln3_d = nc.dram_tensor("ln3_d", [TQ, D], BF16)
    x1q_d = nc.dram_tensor("x1q_d", [TQ, D], F32)
    x2r_d = nc.dram_tensor("x2r_d", [TQ, D], F32)
    x2_d = nc.dram_tensor("x2_d", [TQ, D], F32)
    O_d = nc.dram_tensor("O_d", [D, TQ], BF16)
    CQ_d = nc.dram_tensor("CQ_d", [128, NC1 * TQ], BF16)
    K_loc = nc.dram_tensor("K_loc", [D, TH], BF16)
    V_loc = nc.dram_tensor("V_loc", [TH, D], BF16)
    Kr_loc = nc.dram_tensor("Kr_loc", [DR, TH], BF16)
    U1_16 = nc.dram_tensor("U1_16", [D, RANK], BF16)
    V1_16 = nc.dram_tensor("V1_16", [RANK, 2 * INTER], BF16)
    U2_16 = nc.dram_tensor("U2_16", [INTER, RANK], BF16)
    V2_16 = nc.dram_tensor("V2_16", [RANK, D], BF16)

    dbg = {}
    ffd = {}
    if dev_dump:
        for n, shp in [("d_pt", [128, NR * G]), ("d_act", [128, NI * G]),
                       ("d_zt", [128, NR * G])]:
            ffd[n] = nc.dram_tensor(n, shp, BF16, kind="ExternalOutput").ap()
    if dev_dump:
        for n, src, shp, dt in [
            ("d_ln1", ln1_d, [TH, D], BF16), ("d_ln2", ln2_d, [TH, D], BF16),
            ("d_x1q", x1q_d, [TQ, D], F32), ("d_K", K_sh, [D, S], BF16),
            ("d_V", V_sh, [S, D], BF16), ("d_Kr", Kr_sh, [DR, S], BF16),
            ("d_cq", CQ_d, [128, NC1 * TQ], BF16),
            ("d_O", O_d, [D, TQ], BF16), ("d_x2r", x2r_d, [TQ, D], F32),
            ("d_x2", x2_d, [TQ, D], F32),
        ]:
            dbg[n] = (nc.dram_tensor(n, shp, dt, kind="ExternalOutput").ap(), src, shp)

    ctx = ExitStack()
    e = ctx.enter_context

    bia = e(nc.sbuf_tensor("bia", [128, 269], F32))
    bia_s = e(nc.sbuf_tensor("bia_s", [128, 48], F32))
    CB = {}
    off = 0
    for nm, w in [("dq", 12), ("uq", 32), ("dkv", 4), ("uk", 32), ("rq", 16),
                  ("rk", 1), ("b1g", 86), ("b1u", 86)]:
        CB[nm] = off
        off += w
    ones_bf = e(nc.sbuf_tensor("ones_bf", [128, 1], BF16))
    ones_f32 = e(nc.sbuf_tensor("ones_f32", [128, 1], F32))
    onesr_bf = e(nc.sbuf_tensor("onesr_bf", [1, 128], BF16))
    onesr_f32 = e(nc.sbuf_tensor("onesr_f32", [1, 128], F32))
    brow = e(nc.sbuf_tensor("brow", [1, 2 * D], BF16))
    buv_row = e(nc.sbuf_tensor("buv_row", [1, D], BF16))
    nonce_sb = e(nc.sbuf_tensor("nonce_sb", [1, 4], F32))
    nonce_i = e(nc.sbuf_tensor("nonce_i", [1, 4], I32))
    flag_sb = e(nc.sbuf_tensor("flag_sb", [1, 2], F32))
    flag_i = e(nc.sbuf_tensor("flag_i", [1, 2], I32))

    PSA = e(nc.psum_tensor("PSA", [128, 2048], F32))
    PSB = e(nc.psum_tensor("PSB", [128, 2048], F32))

    mxacc = e(nc.sbuf_tensor("mxacc", [1, 64], F32))
    sems = {}
    for sn in ["dW", "dT", "dX", "dS", "dO", "pe", "ve", "se", "po", "bar",
               "bw", "ph"]:
        sems[sn] = Sem(nc, ctx, sn)
    sU1 = [Sem(nc, ctx, f"sU1{i}") for i in range(2)]
    sV1 = [Sem(nc, ctx, f"sV1{i}") for i in range(2)]
    sU2 = [Sem(nc, ctx, f"sU2{i}") for i in range(2)]
    sV2 = [Sem(nc, ctx, f"sV2{i}") for i in range(2)]
    sLN = [Sem(nc, ctx, f"sLN{i}") for i in range(6)]
    sXT = Sem(nc, ctx, "sXT")
    sUK = [Sem(nc, ctx, f"sUK{i}") for i in range(2)]
    sKT = [Sem(nc, ctx, f"sKT{i}") for i in range(2)]
    sWO = [Sem(nc, ctx, f"sWO{i}") for i in range(2)]
    sPC = [Sem(nc, ctx, f"sPC{i}") for i in range(2)]
    sWS = [Sem(nc, ctx, f"sWS{i}") for i in range(2)]
    sRK = Sem(nc, ctx, "sRK")
    sWQ = [Sem(nc, ctx, f"sWQ{i}") for i in range(2)]
    dW, dT, dX, dS, dO = (sems[k] for k in ["dW", "dT", "dX", "dS", "dO"])
    pe, ve, se, po_s, ph = (sems[k] for k in ["pe", "ve", "se", "po", "ph"])

    class Rr:
        pass

    r = Rr()
    r.t, r.v, r.s, r.g, r.y = (EngRec(x) for x in "tvsgy")
    ALL = (r.t, r.v, r.s, r.g, r.y)

    def join():
        for eng in ALL:
            eng.q("nop").then_inc(ph.inc(), 1)
        tgt = ph.n
        for eng in ALL:
            eng.q("wait_ge", ph.h, tgt)

    def ln_pass(src_f32, dst_bf, sqt, mrow, nrow):
        r.v.q("reduce_sum", out=mrow[:, 0:1], in_=src_f32,
              axis=AX.X).then_inc(ve.inc(), 1)
        r.s.q("wait_ge", ve.h, ve.n)
        r.s.q("square", out=sqt[:, :], in_=src_f32).then_inc(se.inc(), 1)
        r.v.q("wait_ge", se.h, se.n)
        r.v.q("reduce_sum", out=mrow[:, 1:2], in_=sqt[:, :], axis=AX.X)
        r.v.q("tensor_scalar", out=nrow[:, 0:1], in0=mrow[:, 0:1],
              scalar1=1.0 / D, scalar2=None, op0=OP.mult).then_inc(ve.inc(), 1)
        r.v.q("wait_ge", ve.h, ve.n)
        r.v.q("tensor_tensor", out=nrow[:, 1:2], in0=nrow[:, 0:1],
              in1=nrow[:, 0:1], op=OP.mult)
        r.v.q("tensor_scalar", out=nrow[:, 2:3], in0=mrow[:, 1:2],
              scalar1=1.0 / D, scalar2=None, op0=OP.mult).then_inc(ve.inc(), 1)
        r.v.q("wait_ge", ve.h, ve.n)
        r.v.q("scalar_tensor_tensor", out=nrow[:, 3:4], in0=nrow[:, 2:3],
              scalar=1e-5, in1=nrow[:, 1:2], op0=OP.add,
              op1=OP.subtract).then_inc(ve.inc(), 1)
        r.s.q("wait_ge", ve.h, ve.n)
        r.s.q("activation", out=nrow[:, 4:5], in_=nrow[:, 3:4],
              func=AF.Ln).then_inc(se.inc(), 1)
        r.s.q("wait_ge", se.h, se.n)
        r.s.q("activation", out=nrow[:, 5:6], in_=nrow[:, 4:5], func=AF.Exp,
              scale=-0.5).then_inc(se.inc(), 1)
        r.v.q("wait_ge", se.h, se.n)
        r.v.q("tensor_tensor", out=nrow[:, 6:7], in0=nrow[:, 5:6],
              in1=nrow[:, 0:1], op=OP.mult).then_inc(ve.inc(), 1)
        r.v.q("wait_ge", ve.h, ve.n)
        r.v.q("tensor_scalar", out=nrow[:, 7:8], in0=nrow[:, 6:7],
              scalar1=-1.0, scalar2=None, op0=OP.mult).then_inc(ve.inc(), 1)
        r.v.q("wait_ge", ve.h, ve.n)
        r.v.q("tensor_scalar", out=dst_bf, in0=src_f32, scalar1=nrow[:, 5:6],
              scalar2=nrow[:, 7:8], op0=OP.mult, op1=OP.add)

    r.v.q("memset", ones_bf[:, :], 1.0)
    r.v.q("memset", ones_f32[:, :], 1.0)
    r.v.q("memset", onesr_bf[:, :], 1.0)
    r.v.q("memset", onesr_f32[:, :], 1.0)
    r.v.q("memset", mxacc[:, :], 0.0)
    r.y.q("dma_start", out=nonce_sb[:, :], in_=ap["nonce"]).then_inc(dX.inc(16), 16)
    r.y.q("dma_start", out=bia[:, :], in_=ap["bcols"]).then_inc(dX.inc(16), 16)
    r.g.q("dma_start", out=brow[0:1, 0:D], in_=ap["b2"]).then_inc(dW.inc(16), 16)
    r.g.q("dma_start", out=brow[0:1, D:2 * D], in_=ap["b_out"]).then_inc(dW.inc(16), 16)
    r.g.q("dma_start", out=buv_row[0:1, :], in_=ap["b_uv"]).then_inc(dW.inc(16), 16)
    r.v.q("wait_ge", dX.h, dX.n)
    r.v.q("tensor_scalar", out=bia_s[:, 0:32], in0=bia[:, CB["uq"]:CB["uq"] + 32],
          scalar1=SC, scalar2=None, op0=OP.mult)
    r.v.q("tensor_scalar", out=bia_s[:, 32:48], in0=bia[:, CB["rq"]:CB["rq"] + 16],
          scalar1=SC, scalar2=None, op0=OP.mult)
    r.v.q("tensor_copy", out=nonce_i[:, :], in_=nonce_sb[:, :]).then_inc(ve.inc(), 1)

    wstage_cm = nc.sbuf_tensor("wstage", [128, 2 * INTER], BF16)
    ws = wstage_cm.__enter__()

    def cache16(src, dst, rows, cols):
        for r0 in range(0, rows, 128):
            r.g.q("dma_start", out=ws[:, 0:cols],
                  in_=src[r0:r0 + 128, :]).then_inc(dW.inc(16), 16)
            r.g.q("wait_ge", dW.h, dW.n)
            r.g.q("dma_start", out=dst[r0:r0 + 128, :],
                  in_=ws[:, 0:cols]).then_inc(dW.inc(16), 16)
            r.g.q("wait_ge", dW.h, dW.n)

    cache16(ap["U1"], U1_16, D, RANK)
    cache16(ap["V1"], V1_16, RANK, 2 * INTER)
    cache16(ap["U2"], U2_16, INTER, RANK)
    cache16(ap["V2"], V2_16, RANK, D)
    r.g.q("wait_ge", dW.h, dW.n)
    join()
    wstage_cm.__exit__(None, None, None)

    _ffn_n = [0]

    def ffn_call(T, x_src_d, ln_src_d, out_dst_d, ffn1, dump_g0=False):
        _ffn_n[0] += 1
        sfx = f"_f{_ffn_n[0]}"
        ngr = T // G
        with ExitStack() as fx:
            fe = fx.enter_context
            LROT = 6
            lnrot = fe(nc.sbuf_tensor("lnrot" + sfx, [128, LROT * G], BF16))
            Pt = fe(nc.sbuf_tensor("Pt" + sfx, [128, NR * G], BF16))
            act = fe(nc.sbuf_tensor("act" + sfx, [128, NI * G], BF16))
            Zt = fe(nc.sbuf_tensor("Zt" + sfx, [128, NR * G], BF16))
            u1s = fe(nc.sbuf_tensor("u1s" + sfx, [128, 2 * RANK], BF16))
            v1s = fe(nc.sbuf_tensor("v1s" + sfx, [128, 4 * NR * 128], BF16))
            NIH = (NI + 3) // 4
            u2s = fe(nc.sbuf_tensor("u2s" + sfx, [128, 2 * NIH * 128], BF16))
            v2s = fe(nc.sbuf_tensor("v2s" + sfx, [128, 2 * 2048], BF16))
            xt = fe(nc.sbuf_tensor("xt" + sfx, [128, D], F32))
            ot = fe(nc.sbuf_tensor("ot" + sfx, [128, D], F32))
            silu_t = fe(nc.sbuf_tensor("silu_t" + sfx, [128, G], F32))
            sqt = xt
            lnt = fe(nc.sbuf_tensor("lnt_f" + sfx, [128, D], BF16))
            mrow = fe(nc.sbuf_tensor("mrow_f" + sfx, [128, 2], F32))
            nrow = fe(nc.sbuf_tensor("nrow_f" + sfx, [128, 8], F32))

            for gidx in range(ngr):
                def ln_req(k, g_=None):
                    g_ = gidx if g_ is None else g_
                    sl = k % LROT
                    r.y.q("dma_start",
                          out=lnrot[:, sl * G:(sl + 1) * G],
                          in_=ln_src_d[g_ * G:(g_ + 1) * G, k * 128:(k + 1) * 128],
                          transpose=True).then_inc(sLN[sl].inc(16), 16)

                pe_p0 = pe.n
                for k in range(min(LROT, ND)):
                    ln_req(k)
                for k in range(2):
                    r.y.q("dma_start", out=u1s[:, k * RANK:(k + 1) * RANK],
                          in_=U1_16[k * 128:(k + 1) * 128, :]).then_inc(sU1[k % 2].inc(16), 16)
                for k in range(ND):
                    r.t.q("wait_ge", sLN[k % LROT].h, sLN[k % LROT].n)
                    r.t.q("wait_ge", sU1[k % 2].h, sU1[k % 2].n)
                    for m in range(NR):
                        ps = PSA if m < 4 else PSB
                        mo = (m % 4) * G
                        sl = (k % 2) * RANK
                        MM(r.t, ps[:, mo:mo + G],
                           u1s[:, sl + m * 128:sl + (m + 1) * 128],
                           lnrot[:, (k % LROT) * G:(k % LROT + 1) * G],
                           start=(k == 0), stop=(k == ND - 1))
                    r.t.ops[-1].then_inc(pe.inc(), 1)
                    if k + LROT < ND:
                        r.y.q("wait_ge", pe.h, pe_p0 + k + 1)
                        ln_req(k + LROT)
                    if k + 2 < ND:
                        r.g.q("wait_ge", pe.h, pe_p0 + k + 1)
                        r.g.q("dma_start",
                              out=u1s[:, (k % 2) * RANK:(k % 2 + 1) * RANK],
                              in_=U1_16[(k + 2) * 128:(k + 3) * 128, :]).then_inc(sU1[k % 2].inc(16), 16)
                r.v.q("wait_ge", pe.h, pe.n)
                for m in range(NR):
                    ps = PSA if m < 4 else PSB
                    mo = (m % 4) * G
                    r.v.q("tensor_copy", out=Pt[:, m * G:(m + 1) * G],
                          in_=ps[:, mo:mo + G])
                r.v.ops[-1].then_inc(ve.inc(), 1)
                if dump_g0 and gidx == 0:
                    r.y.q("wait_ge", ve.h, ve.n)
                    r.y.q("dma_start", out=ffd["d_pt"], in_=Pt[:, :]).then_inc(dS.inc(16), 16)

                def v1_req(i):
                    sl = (i % 2) * 2 * NR * 128
                    o3, i3 = T3(v1s[:, sl:sl + NR * 128],
                                V1_16[:, i * 128:(i + 1) * 128], NR)
                    r.y.q("dma_start", out=o3, in_=i3).then_inc(sV1[i % 2].inc(16), 16)
                    o3, i3 = T3(v1s[:, sl + NR * 128:sl + 2 * NR * 128],
                                V1_16[:, INTER + i * 128:INTER + (i + 1) * 128], NR)
                    r.y.q("dma_start", out=o3, in_=i3).then_inc(sV1[i % 2].inc(16), 16)

                r.y.q("wait_ge", pe.h, pe.n)
                v1_req(0)
                v1_req(1)
                r.t.q("wait_ge", ve.h, ve.n)
                pe_a0, se_a0, ve_a0 = pe.n, se.n, ve.n
                for i in range(NI):
                    sl = (i % 2) * 2 * NR * 128
                    gp = PSA[:, (i % 2) * G:(i % 2 + 1) * G]
                    up = PSB[:, (i % 2) * G:(i % 2 + 1) * G]
                    r.t.q("wait_ge", sV1[i % 2].h, sV1[i % 2].n)
                    if i >= 2:
                        r.t.q("wait_ge", ve.h, ve_a0 + (i - 1))
                    for c in range(NR):
                        MM(r.t, gp, v1s[:, sl + c * 128:sl + (c + 1) * 128],
                           Pt[:, c * G:(c + 1) * G],
                           start=(c == 0), stop=(c == NR - 1))
                    for c in range(NR):
                        MM(r.t, up,
                           v1s[:, sl + NR * 128 + c * 128:sl + NR * 128 + (c + 1) * 128],
                           Pt[:, c * G:(c + 1) * G],
                           start=(c == 0), stop=(c == NR - 1))
                    r.t.ops[-1].then_inc(pe.inc(), 1)
                    if i + 2 < NI:
                        r.y.q("wait_ge", pe.h, pe_a0 + i + 1)
                        v1_req(i + 2)
                    r.s.q("wait_ge", pe.h, pe_a0 + i + 1)
                    if i >= 1:
                        r.s.q("wait_ge", ve.h, ve_a0 + i)
                    r.s.q("activation", out=silu_t[:, :], in_=gp, func=AF.Silu,
                          bias=bia[:, CB["b1g"] + i:CB["b1g"] + i + 1]).then_inc(se.inc(), 1)
                    r.v.q("wait_ge", se.h, se_a0 + i + 1)
                    r.v.q("scalar_tensor_tensor", out=act[:, i * G:(i + 1) * G],
                          in0=up, scalar=bia[:, CB["b1u"] + i:CB["b1u"] + i + 1],
                          in1=silu_t[:, :], op0=OP.add,
                          op1=OP.mult).then_inc(ve.inc(), 1)

                NQ4 = 4

                def u2_q_req(m, hh, slot):
                    o = slot * NIH * 128
                    lo, hi = hh * NIH * 128, min(NI * 128, (hh + 1) * NIH * 128)
                    o3, i3 = T3(u2s[:, o:o + (hi - lo)],
                                U2_16[lo:hi, m * 128:(m + 1) * 128], (hi - lo) // 128)
                    r.y.q("dma_start", out=o3, in_=i3).then_inc(sU2[slot].inc(16), 16)

                r.t.q("wait_ge", ve.h, ve.n)
                if dump_g0 and gidx == 0:
                    r.y.q("wait_ge", ve.h, ve.n)
                    r.y.q("dma_start", out=ffd["d_act"], in_=act[:, :]).then_inc(dS.inc(16), 16)
                r.y.q("wait_ge", pe.h, pe.n)
                u2_q_req(0, 0, 0)
                u2_q_req(0, 1, 1)
                ve_z0, pe_z0 = ve.n, pe.n
                part_done_pe = []
                hcount = 2
                for m in range(NR):
                    zp = PSA[:, (m % 2) * G:(m % 2 + 1) * G]
                    if m >= 2:
                        r.t.q("wait_ge", ve.h, ve_z0 + (m - 1))
                    for hh in range(NQ4):
                        idx = m * NQ4 + hh
                        slot = idx % 2
                        r.t.q("wait_ge", sU2[slot].h, sU2[slot].n)
                        lo, hi = hh * NIH, min(NI, (hh + 1) * NIH)
                        o = slot * NIH * 128
                        for i in range(lo, hi):
                            MM(r.t, zp, u2s[:, o + (i - lo) * 128:o + (i - lo + 1) * 128],
                               act[:, i * G:(i + 1) * G],
                               start=(i == 0), stop=(i == NI - 1))
                        r.t.ops[-1].then_inc(pe.inc(), 1)
                        part_done_pe.append(pe.n)
                        if hcount < NQ4 * NR:
                            nm, nh = divmod(hcount, NQ4)
                            r.y.q("wait_ge", pe.h, part_done_pe[hcount - 2])
                            u2_q_req(nm, nh, hcount % 2)
                            hcount += 1
                    r.v.q("wait_ge", pe.h, pe.n)
                    r.v.q("tensor_copy", out=Zt[:, m * G:(m + 1) * G],
                          in_=zp).then_inc(ve.inc(), 1)
                r.t.q("wait_ge", ve.h, ve.n)
                if dump_g0 and gidx == 0:
                    r.y.q("wait_ge", ve.h, ve.n)
                    r.y.q("dma_start", out=ffd["d_zt"], in_=Zt[:, :]).then_inc(dS.inc(16), 16)

                ntt = G // 128
                for tt in range(ntt):
                    row0 = gidx * G + tt * 128
                    r.y.q("wait_ge", ve.h, ve.n)
                    r.y.q("dma_start", out=xt[:, :],
                          in_=x_src_d[row0:row0 + 128, :]).then_inc(sXT.inc(16), 16)
                    for hf in range(2):
                        pp = PSA if hf == 0 else PSB
                        r.y.q("wait_ge", pe.h, pe.n)
                        for c in range(2):
                            r.y.q("dma_start",
                                  out=v2s[:, c * 2048:(c + 1) * 2048],
                                  in_=V2_16[c * 128:(c + 1) * 128,
                                            hf * 2048:(hf + 1) * 2048]).then_inc(sV2[c % 2].inc(16), 16)
                        if tt > 0 or hf > 0:
                            r.t.q("wait_ge", ve.h, ve.n)
                        pe_v0 = pe.n
                        for c in range(NR):
                            r.t.q("wait_ge", sV2[c % 2].h, sV2[c % 2].n)
                            MM(r.t, pp[:, 0:2048],
                               Zt[:, c * G + tt * 128:c * G + (tt + 1) * 128],
                               v2s[:, (c % 2) * 2048:(c % 2) * 2048 + 2048],
                               start=(c == 0), stop=False)
                            r.t.ops[-1].then_inc(pe.inc(), 1)
                            if c + 2 < NR:
                                r.y.q("wait_ge", pe.h, pe_v0 + c + 1)
                                r.y.q("dma_start",
                                      out=v2s[:, (c % 2) * 2048:(c % 2 + 1) * 2048],
                                      in_=V2_16[(c + 2) * 128:(c + 3) * 128,
                                                hf * 2048:(hf + 1) * 2048]).then_inc(sV2[c % 2].inc(16), 16)
                        MM(r.t, pp[:, 0:2048], onesr_bf[0:1, :],
                           brow[0:1, hf * 2048:(hf + 1) * 2048],
                           start=False, stop=True)
                        r.t.ops[-1].then_inc(pe.inc(), 1)
                        r.v.q("wait_ge", pe.h, pe.n)
                        r.v.q("wait_ge", sXT.h, sXT.n)
                        r.v.q("scalar_tensor_tensor",
                              out=ot[:, hf * 2048:(hf + 1) * 2048],
                              in0=pp[:, 0:2048], scalar=0.5,
                              in1=xt[:, hf * 2048:(hf + 1) * 2048],
                              op0=OP.mult, op1=OP.add).then_inc(ve.inc(), 1)
                    r.y.q("wait_ge", ve.h, ve.n)
                    if out_dst_d is not None:
                        r.y.q("dma_start", out=out_dst_d[row0:row0 + 128, :],
                              in_=ot[:, :]).then_inc(dO.inc(16), 16)
                    if ffn1:
                        if row0 < TQ:
                            r.y.q("dma_start", out=x1q_d[row0:row0 + 128, :],
                                  in_=ot[:, :]).then_inc(dO.inc(16), 16)
                        r.v.q("wait_ge", dO.h, dO.n)
                        ln_pass(ot[:, :], lnt[:, :], sqt, mrow, nrow)
                        r.v.ops[-1].then_inc(ve.inc(), 1)
                        r.y.q("wait_ge", ve.h, ve.n)
                        r.y.q("dma_start", out=ln2_d[row0:row0 + 128, :],
                              in_=lnt[:, :]).then_inc(dO.inc(16), 16)

    if n_phases >= 1:
        with ExitStack() as px:
            pf = px.enter_context
            xint = pf(nc.sbuf_tensor("xint", [128, D], F32))
            sqt1 = pf(nc.sbuf_tensor("sqt1", [128, D], F32))
            lnt1 = pf(nc.sbuf_tensor("lnt1", [128, D], BF16))
            mr1 = pf(nc.sbuf_tensor("mr1", [128, 2], F32))
            nr1 = pf(nc.sbuf_tensor("nr1", [128, 8], F32))
            for ti in range(TH // 128):
                r.y.q("wait_ge", ve.h, ve.n)
                r.y.q("dma_start", out=xint[:, :],
                      in_=ap["xh"][ti * 128:(ti + 1) * 128, :]).then_inc(dX.inc(16), 16)
                r.v.q("wait_ge", dX.h, dX.n)
                r.v.q("wait_ge", dO.h, dO.n)
                ln_pass(xint[:, :], lnt1[:, :], sqt1, mr1, nr1)
                r.v.ops[-1].then_inc(ve.inc(), 1)
                r.y.q("wait_ge", ve.h, ve.n)
                r.y.q("dma_start", out=ln1_d[ti * 128:(ti + 1) * 128, :],
                      in_=lnt1[:, :]).then_inc(dO.inc(16), 16)
            r.y.q("wait_ge", dO.h, dO.n)
        join()
        if n_phases >= 1.0:
            ffn_call(TH, ap["xh"], ln1_d, None, ffn1=True, dump_g0=dev_dump)
            r.y.q("wait_ge", dO.h, dO.n)
            join()

    CQ_cm = None
    if n_phases >= 1.6:
        def setup_regs(tag, TH_=TH):
            def fn(eng, regs):
                st = ExitStack()
                kr = st.enter_context(eng.register("koff_" + tag))
                pr = st.enter_context(eng.register("par_" + tag))
                eng.reg_mov(pr, eng.to_reg(eng.partition_id()))
                eng.reg_alu(pr, pr, 1, OP.bitwise_and)
                eng.reg_alu(kr, pr, TH_, OP.mult)
                regs["koff_" + tag] = eng.snap(kr, min_val=0, max_val=TH_)
                regs["par_" + tag] = eng.snap(pr, min_val=0, max_val=1)
                regs["_st_" + tag] = st
            return fn

        r.g.raw(setup_regs("g"))

        CKV_cm = nc.sbuf_tensor("CKV", [128, NCC * TH], BF16)
        CKV = CKV_cm.__enter__()
        CQ_cm = nc.sbuf_tensor("CQ", [128, NC1 * TQ], BF16)
        CQ = CQ_cm.__enter__()
        B2 = min(1024, TQ)

        with ExitStack() as px:
            pf = px.enter_context
            lnc = pf(nc.sbuf_tensor("lnc", [128, ND * B2], BF16))
            wsl = pf(nc.sbuf_tensor("wsl", [128, 2 * ND * 128], BF16))
            wrksl = pf(nc.sbuf_tensor("wrksl", [128, ND * 64], BF16))
            rkb = pf(nc.sbuf_tensor("rkb", [64, 1024], BF16))
            sqr_a = pf(nc.sbuf_tensor("sqr_a", [64, 1024], BF16))
            mrow2 = pf(nc.sbuf_tensor("mrow2", [1, 8], F32))

            WSRC = {"dq": ap["w_dq"], "dkv": ap["w_dkv"], "rk": ap["w_rk"]}

            def slab_req(j, kind, m):
                mw = 64 if kind == "rk" else 128
                dst = wrksl[:, 0:ND * 64] if kind == "rk" else \
                    wsl[:, (j % 2) * ND * 128:(j % 2) * ND * 128 + ND * 128]
                o3, i3 = T3(dst, WSRC[kind][:, m * mw:(m + 1) * mw], ND)
                sm = sRK if kind == "rk" else sWS[j % 2]
                r.g.q("dma_start", out=o3, in_=i3).then_inc(sm.inc(16), 16)

            for th4 in range(TH // B2):
                dT_0 = dT.n
                r.y.q("wait_ge", pe.h, pe.n)
                r.g.q("wait_ge", pe.h, pe.n)
                for k in range(ND):
                    r.y.q("dma_start", out=lnc[:, k * B2:(k + 1) * B2],
                          in_=ln2_d[th4 * B2:(th4 + 1) * B2, k * 128:(k + 1) * 128],
                          transpose=True).then_inc(dT.inc(16), 16)
                mlist = ([("dq", m) for m in range(NC1)] if th4 < TQ // B2 else [])
                mlist += [("dkv", m) for m in range(NCC)] + [("rk", 0)]
                for j in range(min(2, len(mlist))):
                    slab_req(j, *mlist[j])
                r.t.q("wait_ge", dT.h, dT_0 + 16 * ND)
                pe_0 = pe.n
                evac_ve = {}
                for j, (kind, m) in enumerate(mlist):
                    smw = sRK if kind == "rk" else sWS[j % 2]
                    r.t.q("wait_ge", smw.h, smw.n)
                    mw = 64 if kind == "rk" else 128
                    slab = wrksl[:, 0:ND * 64] if kind == "rk" else \
                        wsl[:, (j % 2) * ND * 128:(j % 2) * ND * 128 + ND * 128]
                    pp = PSB[0:64, 0:B2] if kind == "rk" else \
                        PSA[:, (j % 2) * B2:(j % 2) * B2 + B2]
                    if j >= 2 and kind != "rk":
                        r.t.q("wait_ge", ve.h, evac_ve[j - 2])
                    for k in range(ND):
                        MM(r.t, pp, slab[:, k * mw:(k + 1) * mw],
                           lnc[:, k * B2:(k + 1) * B2],
                           start=(k == 0), stop=(k == ND - 1))
                    r.t.ops[-1].then_inc(pe.inc(), 1)
                    if j + 2 < len(mlist):
                        r.g.q("wait_ge", pe.h, pe_0 + j + 1)
                        slab_req(j + 2, *mlist[j + 2])
                    r.v.q("wait_ge", pe.h, pe_0 + j + 1)
                    if kind == "dq":
                        r.v.q("tensor_scalar",
                              out=CQ[:, m * TQ + th4 * B2:m * TQ + (th4 + 1) * B2],
                              in0=pp, scalar1=bia[:, CB["dq"] + m:CB["dq"] + m + 1],
                              scalar2=None, op0=OP.add).then_inc(ve.inc(), 1)
                    elif kind == "dkv":
                        r.v.q("tensor_scalar",
                              out=CKV[:, m * TH + th4 * B2:m * TH + (th4 + 1) * B2],
                              in0=pp, scalar1=bia[:, CB["dkv"] + m:CB["dkv"] + m + 1],
                              scalar2=None, op0=OP.add).then_inc(ve.inc(), 1)
                    else:
                        r.v.q("wait_ge", dS.h, dS.n)
                        r.v.q("tensor_scalar", out=rkb[0:64, 0:B2], in0=pp,
                              scalar1=bia[0:64, CB["rk"]:CB["rk"] + 1],
                              scalar2=None, op0=OP.add).then_inc(ve.inc(), 1)
                        r.y.q("wait_ge", ve.h, ve.n)
                        r.y.q("dma_start",
                              out=Kr_loc.ap()[:, th4 * B2:(th4 + 1) * B2],
                              in_=rkb[0:64, 0:B2]).then_inc(dS.inc(16), 16)
                        r.s.q("wait_ge", ve.h, ve.n)
                        r.s.q("square", out=sqr_a[0:64, 0:B2],
                              in_=rkb[0:64, 0:B2]).then_inc(se.inc(), 1)
                        r.t.q("wait_ge", se.h, se.n)
                        if th4 >= 1:
                            r.t.q("wait_ge", ve.h, ve.n)
                        for jj in range(B2 // 512):
                            r.t.q("matmul",
                                  PSB[0:1, B2 + jj * 512:B2 + (jj + 1) * 512],
                                  lhsT=ones_bf[0:64, :],
                                  rhs=sqr_a[0:64, jj * 512:(jj + 1) * 512],
                                  start=True, stop=True)
                        r.t.ops[-1].then_inc(pe.inc(), 1)
                        r.v.q("wait_ge", pe.h, pe.n)
                        r.v.q("reduce_max", out=mrow2[0:1, th4:th4 + 1],
                              in_=PSB[0:1, B2:B2 + B2][0:1, 0:B2],
                              axis=AX.X).then_inc(ve.inc(), 1)
                    evac_ve[j] = ve.n
            r.v.q("wait_ge", ve.h, ve.n)
            r.v.q("reduce_max", out=mxacc[0:1, 32:33],
                  in_=mrow2[0:1, 0:TH // B2], axis=AX.X).then_inc(ve.inc(), 1)
            for c in range(NC1):
                r.y.q("wait_ge", ve.h, ve.n)
                r.y.q("dma_start", out=CQ_d[:, c * TQ:(c + 1) * TQ],
                      in_=CQ[:, c * TQ:(c + 1) * TQ]).then_inc(dS.inc(16), 16)
            r.y.q("wait_ge", dS.h, dS.n)
            join()
        CQ_cm.__exit__(None, None, None)

        if n_phases >= 1.7:
            with ExitStack() as px:
                pf = px.enter_context
                uksl = pf(nc.sbuf_tensor("uksl", [128, 2 * NCC * 128], BF16))
                ktile = pf(nc.sbuf_tensor("ktile", [128, 2 * 1024], BF16))
                sqk = pf(nc.sbuf_tensor("sqk", [128, 1024], BF16))
                mrow3 = pf(nc.sbuf_tensor("mrow3", [1, 8], F32))

                def uk_req(m):
                    o = (m % 2) * NCC * 128
                    o3, i3 = T3(uksl[:, o:o + NCC * 128],
                                ap["w_uk"][:, m * 128:(m + 1) * 128], NCC)
                    r.g.q("dma_start", out=o3, in_=i3).then_inc(sUK[m % 2].inc(16), 16)

                uk_req(0)
                uk_req(1)
                pe_0 = pe.n
                NKH = TH // 1024
                step = 0
                ve_ev = {}
                se_sq = {}
                for m in range(ND):
                    r.t.q("wait_ge", sUK[m % 2].h, sUK[m % 2].n)
                    o = (m % 2) * NCC * 128
                    if m + 2 < ND:
                        r.g.q("wait_ge", pe.h, pe_0 + 2 * (m + 1) * NKH)
                        uk_req(m + 2)
                    for kh in range(NKH):
                        pp = PSA[:, (step % 2) * 1024:(step % 2) * 1024 + 1024]
                        if step >= 2:
                            r.t.q("wait_ge", ve.h, ve_ev[step - 2])
                        for c in range(NCC):
                            MM(r.t, pp, uksl[:, o + c * 128:o + (c + 1) * 128],
                               CKV[:, c * TH + kh * 1024:c * TH + (kh + 1) * 1024],
                               start=(c == 0), stop=(c == NCC - 1))
                        r.t.ops[-1].then_inc(pe.inc(), 1)
                        r.v.q("wait_ge", pe.h, pe.n)
                        r.v.q("wait_ge", sKT[step % 2].h, sKT[step % 2].n)
                        r.v.q("tensor_scalar",
                              out=ktile[:, (step % 2) * 1024:(step % 2) * 1024 + 1024],
                              in0=pp, scalar1=bia[:, CB["uk"] + m:CB["uk"] + m + 1],
                              scalar2=None, op0=OP.add).then_inc(ve.inc(), 1)
                        ve_ev[step] = ve.n
                        r.y.q("wait_ge", ve.h, ve.n)
                        r.y.q("dma_start",
                              out=K_loc.ap()[m * 128:(m + 1) * 128,
                                             kh * 1024:(kh + 1) * 1024],
                              in_=ktile[:, (step % 2) * 1024:(step % 2) * 1024 + 1024]
                              ).then_inc(sKT[step % 2].inc(16), 16)
                        r.s.q("wait_ge", ve.h, ve_ev[step])
                        if step >= 1:
                            r.s.q("wait_ge", pe.h, pe.n)
                        r.s.q("square", out=sqk[:, :],
                              in_=ktile[:, (step % 2) * 1024:(step % 2) * 1024 + 1024]
                              ).then_inc(se.inc(), 1)
                        se_sq[step] = se.n
                        r.t.q("wait_ge", se.h, se_sq[step])
                        if step >= 1:
                            r.t.q("wait_ge", ve.h, ve.n)
                        for jj in range(2):
                            r.t.q("matmul", PSB[0:1, jj * 512:(jj + 1) * 512],
                                  lhsT=ones_bf[:, :],
                                  rhs=sqk[:, jj * 512:(jj + 1) * 512],
                                  start=True, stop=True)
                        r.t.ops[-1].then_inc(pe.inc(), 1)
                        r.v.q("wait_ge", pe.h, pe.n)
                        r.v.q("reduce_max", out=mrow3[0:1, kh:kh + 1],
                              in_=PSB[0:1, 0:1024], axis=AX.X).then_inc(ve.inc(), 1)
                        step += 1
                    r.v.q("wait_ge", ve.h, ve.n)
                    r.v.q("reduce_max", out=mxacc[0:1, m:m + 1],
                          in_=mrow3[0:1, 0:NKH], axis=AX.X).then_inc(ve.inc(), 1)
                r.y.q("wait_ge", ve.h, ve.n)
                r.y.q("wait_ge", dS.h, dS.n)
                r.y.q("wait_ge", sKT[0].h, sKT[0].n)
                r.y.q("wait_ge", sKT[1].h, sKT[1].n)
                join()

        if n_phases >= 1.8:
            with ExitStack() as px:
                pf = px.enter_context
                wuv4 = pf(nc.sbuf_tensor("wuv4", [128, NCC * D], BF16))
                vtile = pf(nc.sbuf_tensor("vtile", [128, D], BF16))
                dW_0 = dW.n
                for c in range(NCC):
                    r.g.q("dma_start", out=wuv4[:, c * D:(c + 1) * D],
                          in_=ap["w_uv"][c * 128:(c + 1) * 128, :]).then_inc(dW.inc(16), 16)
                r.t.q("wait_ge", dW.h, dW_0 + 16 * NCC)
                pe_0 = pe.n
                for kt in range(TH // 128):
                    if kt > 0:
                        r.t.q("wait_ge", ve.h, ve.n)
                    for c in range(NCC):
                        lh = CKV[:, c * TH + kt * 128:c * TH + (kt + 1) * 128]
                        MM(r.t, PSA[:, 0:2048], lh, wuv4[:, c * D:c * D + 2048],
                           start=(c == 0), stop=False)
                        MM(r.t, PSB[:, 0:2048], lh, wuv4[:, c * D + 2048:(c + 1) * D],
                           start=(c == 0), stop=False)
                    MM(r.t, PSA[:, 0:2048], onesr_bf[0:1, :], buv_row[0:1, 0:2048],
                       start=False, stop=True)
                    MM(r.t, PSB[:, 0:2048], onesr_bf[0:1, :], buv_row[0:1, 2048:D],
                       start=False, stop=True)
                    r.t.ops[-1].then_inc(pe.inc(), 1)
                    r.v.q("wait_ge", pe.h, pe_0 + kt + 1)
                    r.v.q("wait_ge", dS.h, dS.n)
                    r.v.q("tensor_copy", out=vtile[:, 0:2048], in_=PSA[:, 0:2048])
                    r.v.q("tensor_copy", out=vtile[:, 2048:D],
                          in_=PSB[:, 0:2048]).then_inc(ve.inc(), 1)
                    r.y.q("wait_ge", ve.h, ve.n)
                    r.y.q("dma_start",
                          out=V_loc.ap()[kt * 128:(kt + 1) * 128, :],
                          in_=vtile[:, :]).then_inc(dS.inc(16), 16)
                r.y.q("wait_ge", dS.h, dS.n)
                join()
            CKV_cm.__exit__(None, None, None)

        if n_phases >= 2:
            pub_ds0 = dS.n
            NKP, NVP = 4, 2
            npub = NKP + NVP + 2
            dS.inc(16 * npub)
            pub_tgt = dS.n
            bw_tgt = sems["bw"].n + 16
            sems["bw"].inc(16)
            dS_h, bw_h = dS.h, sems["bw"].h
            ve_h2, ve_pub = ve.h, ve.n

            def publish(eng, regs):
                eng.wait_ge(ve_h2, ve_pub)
                eng.wait_ge(dS_h, pub_ds0)
                for arm in nc.switch_core_id([eng], n=128):
                    par = (int(arm) % 8) & 1
                    for jj in range(NKP):
                        rr = D // NKP
                        eng.dma_start(
                            out=K_sh.ap()[jj * rr:(jj + 1) * rr,
                                          par * TH:(par + 1) * TH],
                            in_=K_loc.ap()[jj * rr:(jj + 1) * rr, :]).then_inc(dS_h, 16)
                    for jj in range(NVP):
                        rr = TH // NVP
                        eng.dma_start(
                            out=V_sh.ap()[par * TH + jj * rr:par * TH + (jj + 1) * rr, :],
                            in_=V_loc.ap()[jj * rr:(jj + 1) * rr, :]).then_inc(dS_h, 16)
                    eng.dma_start(out=Kr_sh.ap()[:, par * TH:(par + 1) * TH],
                                  in_=Kr_loc.ap()[:, :]).then_inc(dS_h, 16)
                    eng.dma_start(out=mx_sh.ap()[par:par + 1, :],
                                  in_=mxacc[0:1, :]).then_inc(dS_h, 16)
                    eng.wait_ge(dS_h, pub_tgt)
                    eng.dma_start(out=flags.ap()[0:1, par:par + 1],
                                  in_=nonce_sb[0:1, 0:1]).then_inc(bw_h, 16)
                eng.wait_ge(bw_h, bw_tgt)

            r.y.raw(publish)

            r.y.q("nop").then_inc(ph.inc(), 1)
            bar_tgt = ph.n
            ge_h = sems["bar"].h

            def barrier(eng, regs):
                eng.wait_ge(ph.h, bar_tgt)
                st = ExitStack()
                c = st.enter_context(eng.register("bar_c"))
                vv = st.enter_context(eng.register("bar_v"))
                kk = st.enter_context(eng.register("bar_k"))
                ps = st.enter_context(eng.register("bar_p"))
                eng.reg_load(kk, nonce_i[0:1, 0:1])
                eng.reg_mov(ps, regs["par_g"])
                eng.reg_alu(ps, ps, 1, OP.bitwise_xor)
                opar = eng.snap(ps, min_val=0, max_val=1)
                cnt = st.enter_context(eng.register("bar_n"))
                eng.reg_mov(c, 1)
                eng.reg_mov(cnt, 16)
                with eng.While(c):
                    eng.dma_start(out=flag_sb[0:1, 0:2],
                                  in_=flags.ap()[0:1, 0:2]).then_inc(ge_h, 16)
                    eng.wait_ge(ge_h, cnt)
                    eng.reg_alu(cnt, cnt, 16, OP.add)
                    eng.tensor_copy(out=flag_i[0:1, 0:2], in_=flag_sb[0:1, 0:2])
                    eng.reg_load(vv, flag_i[0:1, bass.ds(opar, 1)])
                    eng.reg_alu(c, vv, kk, OP.not_equal)
                st.close()

            r.g.raw(barrier)
            r.g.q("nop").then_inc(ph.inc(), 1)
            for eng in ALL:
                eng.q("wait_ge", ph.h, ph.n)

        if 1.7 <= n_phases < 1.8:
            CKV_cm.__exit__(None, None, None)
        if n_phases < 1.7:
            CKV_cm.__exit__(None, None, None)

    if n_phases >= 3:
        SEG = 2048
        NSEG = S // SEG
        KCS = SEG // 128
        QHW = TQ // 2
        with ExitStack() as px:
            pf = px.enter_context
            KRA = pf(nc.sbuf_tensor("KRA", [65, S], BF16))
            CQ = pf(nc.sbuf_tensor("CQ3", [128, NC1 * TQ], BF16))
            Kb = pf(nc.sbuf_tensor("Kb", [128, 2 * SEG], BF16))
            Vb = pf(nc.sbuf_tensor("Vb", [128, 2 * SEG], BF16))
            Qh = pf(nc.sbuf_tensor("Qh", [128, TQ], BF16))
            qra = pf(nc.sbuf_tensor("qra", [65, TQ], BF16))
            expS = pf(nc.sbuf_tensor("expS", [128, 2 * 1024], BF16))
            accq = pf(nc.sbuf_tensor("accq", [128, TQ], F32))
            invb = pf(nc.sbuf_tensor("invb", [128, TQ], F32))
            Osb = pf(nc.sbuf_tensor("Osb", [128, TQ], BF16))
            wuq2 = pf(nc.sbuf_tensor("wuq2", [128, 2 * NC1 * 128], BF16))
            wrq2 = pf(nc.sbuf_tensor("wrq2", [128, 2 * NC1 * 64], BF16))
            sq32 = pf(nc.sbuf_tensor("sq32", [128, 512], F32))
            sqr2 = pf(nc.sbuf_tensor("sqr2", [64, 512], F32))
            qn2r = pf(nc.sbuf_tensor("qn2r", [1, TQ], F32))
            srow = pf(nc.sbuf_tensor("srow", [1, TQ], F32))
            mxr0 = pf(nc.sbuf_tensor("mxr0", [1, 64], F32))
            mxr1 = pf(nc.sbuf_tensor("mxr1", [1, 64], F32))
            lkr = pf(nc.sbuf_tensor("lkr", [1, 192], F32))
            srow2 = pf(nc.sbuf_tensor("srow2", [1, TQ], F32))

            r.y.q("dma_start", out=KRA[0:64, :],
                  in_=Kr_sh.ap()[:, :]).then_inc(dX.inc(16), 16)
            r.v.q("memset", KRA[64:65, :], 1.0).then_inc(ve.inc(), 1)
            for c in range(NC1):
                r.y.q("dma_start", out=CQ[:, c * TQ:(c + 1) * TQ],
                      in_=CQ_d[:, c * TQ:(c + 1) * TQ]).then_inc(dX.inc(16), 16)
            r.y.q("dma_start", out=mxr0[0:1, :],
                  in_=mx_sh.ap()[0:1, :]).then_inc(dX.inc(16), 16)
            r.y.q("dma_start", out=mxr1[0:1, :],
                  in_=mx_sh.ap()[1:2, :]).then_inc(dX.inc(16), 16)
            pre_dX, pre_ve = dX.n, ve.n
            r.v.q("wait_ge", dX.h, pre_dX)
            r.v.q("tensor_tensor", out=lkr[0:1, 0:64], in0=mxr0[0:1, :],
                  in1=mxr1[0:1, :], op=OP.max).then_inc(ve.inc(), 1)
            r.v.q("wait_ge", ve.h, ve.n)
            r.v.q("tensor_scalar", out=lkr[0:1, 64:96], in0=lkr[0:1, 0:32],
                  scalar1=lkr[0:1, 32:33], scalar2=1e-30,
                  op0=OP.add, op1=OP.add).then_inc(ve.inc(), 1)
            r.s.q("wait_ge", ve.h, ve.n)
            r.s.q("activation", out=lkr[0:1, 96:128], in_=lkr[0:1, 64:96],
                  func=AF.Ln).then_inc(se.inc(), 1)
            r.v.q("wait_ge", se.h, se.n)
            r.v.q("tensor_scalar", out=lkr[0:1, 128:160], in0=lkr[0:1, 96:128],
                  scalar1=0.5, scalar2=None, op0=OP.mult).then_inc(ve.inc(), 1)
            pre_ve = ve.n

            def pf_head(h, wait_pe=None):
                if wait_pe is not None:
                    r.g.q("wait_ge", pe.h, wait_pe)
                o3, i3 = T3(wuq2[:, (h % 2) * NC1 * 128:(h % 2 + 1) * NC1 * 128],
                            ap["w_uq"][:, h * 128:(h + 1) * 128], NC1)
                r.g.q("dma_start", out=o3, in_=i3).then_inc(sWQ[h % 2].inc(16), 16)
                o3, i3 = T3(wrq2[:, (h % 2) * NC1 * 64:(h % 2 + 1) * NC1 * 64],
                            ap["w_rq"][:, h * 64:(h + 1) * 64], NC1)
                r.g.q("dma_start", out=o3, in_=i3).then_inc(sWQ[h % 2].inc(16), 16)
                return sWQ[h % 2].n

            def pf_seg(h, seg, wait_pe=None):
                sl = (seg % 2) * SEG
                if wait_pe is not None:
                    r.y.q("wait_ge", pe.h, wait_pe)
                r.y.q("dma_start", out=Kb[:, sl:sl + SEG],
                      in_=K_sh.ap()[h * 128:(h + 1) * 128,
                                    seg * SEG:(seg + 1) * SEG]).then_inc(dX.inc(16), 16)
                o3, i3 = T3(Vb[:, sl:sl + SEG],
                            V_sh.ap()[seg * SEG:(seg + 1) * SEG,
                                      h * 128:(h + 1) * 128], SEG // 128)
                r.y.q("dma_start", out=o3, in_=i3).then_inc(dX.inc(16), 16)
                return dX.n

            pfw = {0: pf_head(0)}
            seg_dX = {(0, 0): pf_seg(0, 0)}
            evacC_ve = {-1: pre_ve}
            for h in range(H):
                slq = (h % 2) * NC1 * 128
                slr = (h % 2) * NC1 * 64
                r.t.q("wait_ge", sWQ[h % 2].h, pfw[h])
                r.t.q("wait_ge", dX.h, pre_dX)
                r.t.q("wait_ge", ve.h, max(evacC_ve[h - 1], pre_ve))
                for c in range(NC1):
                    MM(r.t, PSA[:, 0:TQ], wuq2[:, slq + c * 128:slq + (c + 1) * 128],
                       CQ[:, c * TQ:(c + 1) * TQ],
                       start=(c == 0), stop=(c == NC1 - 1))
                r.t.ops[-1].then_inc(pe.inc(), 1)
                if h + 1 < H:
                    pfw[h + 1] = pf_head(h + 1, wait_pe=(None if h == 0 else pe.n))
                r.v.q("wait_ge", pe.h, pe.n)
                r.v.q("tensor_scalar", out=Qh[:, :], in0=PSA[:, 0:TQ],
                      scalar1=SC, scalar2=bia_s[:, h:h + 1],
                      op0=OP.mult, op1=OP.add).then_inc(ve.inc(), 1)
                for c in range(NC1):
                    MM(r.t, PSB[0:64, 0:TQ], wrq2[:, slr + c * 64:slr + (c + 1) * 64],
                       CQ[:, c * TQ:(c + 1) * TQ],
                       start=(c == 0), stop=(c == NC1 - 1))
                r.t.ops[-1].then_inc(pe.inc(), 1)
                r.v.q("wait_ge", pe.h, pe.n)
                r.v.q("tensor_scalar", out=qra[0:64, :], in0=PSB[0:64, 0:TQ],
                      scalar1=SC,
                      scalar2=bia_s[64 * (h % 2):64 * (h % 2) + 64,
                                    32 + h // 2:33 + h // 2],
                      op0=OP.mult, op1=OP.add)
                r.v.q("memset", accq[:, :], 0.0).then_inc(ve.inc(), 1)
                ve_qr = ve.n
                pe_n0 = pe.n
                po_n0 = po_s.n
                ve_n0 = ve.n
                NQC = TQ // 512
                for j in range(NQC):
                    r.g.q("wait_ge", ve.h, ve_qr)
                    if j >= 1:
                        r.g.q("wait_ge", pe.h, pe_n0 + j)
                    r.g.q("tensor_tensor", out=sq32[:, :],
                          in0=Qh[:, j * 512:(j + 1) * 512],
                          in1=Qh[:, j * 512:(j + 1) * 512], op=OP.mult)
                    r.g.q("tensor_tensor", out=sqr2[0:64, :],
                          in0=qra[0:64, j * 512:(j + 1) * 512],
                          in1=qra[0:64, j * 512:(j + 1) * 512],
                          op=OP.mult).then_inc(po_s.inc(), 1)
                    r.t.q("wait_ge", po_s.h, po_s.n)
                    if j >= 2:
                        r.t.q("wait_ge", ve.h, ve_n0 + j - 1)
                    pq = PSA[0:1, (j % 2) * 512:(j % 2) * 512 + 512]
                    r.t.q("matmul", pq, lhsT=ones_f32[:, :], rhs=sq32[:, :],
                          start=True, stop=False)
                    r.t.q("matmul", pq, lhsT=ones_f32[0:64, :], rhs=sqr2[0:64, :],
                          start=False, stop=True).then_inc(pe.inc(), 1)
                    r.v.q("wait_ge", pe.h, pe.n)
                    r.v.q("tensor_scalar", out=qn2r[0:1, j * 512:(j + 1) * 512],
                          in0=pq, scalar1=1e-30, scalar2=None,
                          op0=OP.add).then_inc(ve.inc(), 1)
                r.s.q("wait_ge", ve.h, ve.n)
                r.s.q("activation", out=srow[0:1, :], in_=qn2r[0:1, :],
                      func=AF.Ln).then_inc(se.inc(), 1)
                r.v.q("wait_ge", se.h, se.n)
                r.v.q("tensor_scalar", out=srow2[0:1, :], in0=srow[0:1, :],
                      scalar1=0.5, scalar2=lkr[0:1, 128 + h:129 + h],
                      op0=OP.mult, op1=OP.add).then_inc(ve.inc(), 1)
                r.s.q("wait_ge", ve.h, ve.n)
                r.s.q("activation", out=srow[0:1, :], in_=srow2[0:1, :],
                      func=AF.Exp).then_inc(se.inc(), 1)
                r.v.q("wait_ge", se.h, se.n)
                r.v.q("tensor_scalar", out=qra[64:65, :], in0=srow[0:1, :],
                      scalar1=-1.0, scalar2=None, op0=OP.mult).then_inc(ve.inc(), 1)
                ve_mrow = ve.n

                r.t.q("wait_ge", ve.h, ve_mrow)
                scnt = 0
                pe_sc = {}
                pe_av = {}
                se_ex = {}
                ve_sm = {}
                order = []
                for seg in range(NSEG):
                    for qh in range(2):
                        qo = qh * QHW
                        r.t.q("wait_ge", dX.h, seg_dX[(h, seg)])
                        if qh == 1 and NSEG > 1:
                            nh, nseg = (h, seg + 1) if seg + 1 < NSEG else (h + 1, 0)
                            if nh < H and (nh, nseg) not in seg_dX:
                                seg_dX[(nh, nseg)] = pf_seg(nh, nseg, wait_pe=pe.n)
                        for kc in range(KCS):
                            gk = seg * KCS + kc
                            idx = (seg, qh, kc)
                            prev2 = order[-2] if len(order) >= 2 else None
                            prev1 = order[-1] if len(order) >= 1 else None
                            if prev2 is not None:
                                r.t.q("wait_ge", se.h, se_ex[prev2])
                            sps = PSA[:, (scnt % 2) * 1024:(scnt % 2) * 1024 + 1024]
                            MM(r.t, sps[:, 0:QHW],
                               Kb[:, (seg % 2) * SEG + kc * 128:(seg % 2) * SEG + (kc + 1) * 128],
                               Qh[:, qo:qo + QHW], start=True, stop=False)
                            MM(r.t, sps[:, 0:QHW],
                               KRA[:, gk * 128:(gk + 1) * 128],
                               qra[:, qo:qo + QHW], start=False, stop=True)
                            r.t.ops[-1].then_inc(pe.inc(), 1)
                            pe_sc[idx] = pe.n
                            if prev1 is not None:
                                pseg, pqh, pkc = prev1
                                r.t.q("wait_ge", se.h, se_ex[prev1])
                                pgk = pseg * KCS + pkc
                                MM(r.t, PSB[:, pqh * QHW:(pqh + 1) * QHW],
                                   Vb[:, (pseg % 2) * SEG + pkc * 128:(pseg % 2) * SEG + (pkc + 1) * 128],
                                   expS[:, ((scnt - 1) % 2) * 1024:((scnt - 1) % 2) * 1024 + QHW],
                                   start=(pgk == 0), stop=(pgk == NKC - 1))
                                r.t.ops[-1].then_inc(pe.inc(), 1)
                                pe_av[prev1] = pe.n
                            r.s.q("wait_ge", pe.h, pe_sc[idx])
                            if prev2 is not None:
                                r.s.q("wait_ge", pe.h, pe_av[prev2])
                                r.s.q("wait_ge", ve.h, ve_sm[prev2])
                            r.s.q("activation",
                                  out=expS[:, (scnt % 2) * 1024:(scnt % 2) * 1024 + QHW],
                                  in_=sps[:, 0:QHW], func=AF.Exp).then_inc(se.inc(), 1)
                            se_ex[idx] = se.n
                            r.v.q("wait_ge", se.h, se_ex[idx])
                            r.v.q("tensor_tensor", out=accq[:, qo:qo + QHW],
                                  in0=accq[:, qo:qo + QHW],
                                  in1=expS[:, (scnt % 2) * 1024:(scnt % 2) * 1024 + QHW],
                                  op=OP.add).then_inc(ve.inc(), 1)
                            ve_sm[idx] = ve.n
                            order.append(idx)
                            scnt += 1
                prev1 = order[-1]
                pseg, pqh, pkc = prev1
                r.t.q("wait_ge", se.h, se_ex[prev1])
                pgk = pseg * KCS + pkc
                MM(r.t, PSB[:, pqh * QHW:(pqh + 1) * QHW],
                   Vb[:, (pseg % 2) * SEG + pkc * 128:(pseg % 2) * SEG + (pkc + 1) * 128],
                   expS[:, ((scnt - 1) % 2) * 1024:((scnt - 1) % 2) * 1024 + QHW],
                   start=(pgk == 0), stop=True)
                r.t.ops[-1].then_inc(pe.inc(), 1)
                pe_av[prev1] = pe.n
                if h + 1 < H and (h + 1, 0) not in seg_dX:
                    seg_dX[(h + 1, 0)] = pf_seg(h + 1, 0, wait_pe=pe.n)

                r.t.q("wait_ge", ve.h, ve.n)
                for j in range(TQ // 512):
                    r.t.q("matmul", PSA[0:1, j * 512:(j + 1) * 512],
                          lhsT=ones_f32[:, :], rhs=accq[:, j * 512:(j + 1) * 512],
                          start=True, stop=True)
                r.t.ops[-1].then_inc(pe.inc(), 1)
                r.v.q("wait_ge", pe.h, pe.n)
                r.v.q("tensor_copy", out=srow2[0:1, :],
                      in_=PSA[0:1, 0:TQ]).then_inc(ve.inc(), 1)
                r.v.q("wait_ge", ve.h, ve.n)
                r.v.q("reciprocal", out=srow[0:1, :],
                      in_=srow2[0:1, :]).then_inc(ve.inc(), 1)
                r.t.q("wait_ge", ve.h, ve.n)
                for j in range(TQ // 512):
                    r.t.q("matmul", PSA[:, j * 512:(j + 1) * 512],
                          lhsT=onesr_f32[0:1, :],
                          rhs=srow[0:1, j * 512:(j + 1) * 512],
                          start=True, stop=True)
                r.t.ops[-1].then_inc(pe.inc(), 1)
                r.v.q("wait_ge", pe.h, pe.n)
                r.v.q("wait_ge", dS.h, dS.n)
                r.v.q("tensor_copy", out=invb[:, :],
                      in_=PSA[:, 0:TQ]).then_inc(ve.inc(), 1)
                r.v.q("wait_ge", ve.h, ve.n)
                r.v.q("tensor_tensor", out=Osb[:, :], in0=PSB[:, 0:TQ],
                      in1=invb[:, :], op=OP.mult).then_inc(ve.inc(), 1)
                evacC_ve[h] = ve.n
                r.y.q("wait_ge", ve.h, ve.n)
                r.y.q("dma_start", out=O_d[h * 128:(h + 1) * 128, :],
                      in_=Osb[:, :]).then_inc(dS.inc(16), 16)
            r.y.q("wait_ge", dS.h, dS.n)
        join()

    if n_phases >= 4:
        with ExitStack() as px:
            pf = px.enter_context
            Oh = pf(nc.sbuf_tensor("Oh", [128, ND * (TQ // 2)], BF16))
            wosl = pf(nc.sbuf_tensor("wosl", [128, 2 * ND * 512], BF16))
            piece = pf(nc.sbuf_tensor("piece", [128, 2 * 512], F32))
            THW = TQ // 2
            for th in range(2):
                dX_0 = dX.n
                r.y.q("wait_ge", pe.h, pe.n)
                for cq in range(4):
                    o3, i3 = T3(Oh[:, cq * (ND // 4) * THW:(cq + 1) * (ND // 4) * THW],
                                O_d.ap()[cq * (D // 4):(cq + 1) * (D // 4),
                                         th * THW:(th + 1) * THW], ND // 4)
                    r.y.q("dma_start", out=o3, in_=i3).then_inc(dX.inc(16), 16)
                r.g.q("wait_ge", pe.h, pe.n)
                o3, i3 = T3(wosl[:, 0:ND * 512], ap["w_out"][:, 0:512], ND)
                r.g.q("dma_start", out=o3, in_=i3).then_inc(sWO[0].inc(16), 16)
                r.t.q("wait_ge", dX.h, dX_0 + 64)
                pe_0 = pe.n
                step = 0
                for dc in range(ND // 4):
                    if dc + 1 < ND // 4:
                        r.g.q("wait_ge", pe.h, pe_0 + step)
                        o3, i3 = T3(wosl[:, ((dc + 1) % 2) * ND * 512:((dc + 1) % 2 + 1) * ND * 512],
                                    ap["w_out"][:, (dc + 1) * 512:(dc + 2) * 512], ND)
                        r.g.q("dma_start", out=o3, in_=i3).then_inc(sWO[(dc + 1) % 2].inc(16), 16)
                    r.t.q("wait_ge", sWO[dc % 2].h, sWO[dc % 2].n)
                    osl = (dc % 2) * ND * 512
                    for tt in range(THW // 128):
                        pp = PSA[:, (tt % 2) * 512:(tt % 2) * 512 + 512]
                        if step >= 2:
                            r.t.q("wait_ge", ve.h, ve.n - 1)
                        for hc in range(ND):
                            r.t.q("matmul", pp,
                                  lhsT=Oh[:, hc * THW + tt * 128:hc * THW + (tt + 1) * 128],
                                  rhs=wosl[:, osl + hc * 512:osl + (hc + 1) * 512],
                                  start=(hc == 0), stop=False)
                        r.t.q("matmul", pp, lhsT=onesr_bf[0:1, :],
                              rhs=brow[0:1, D + dc * 512:D + (dc + 1) * 512],
                              start=False, stop=True).then_inc(pe.inc(), 1)
                        r.v.q("wait_ge", pe.h, pe.n)
                        r.v.q("wait_ge", sPC[tt % 2].h, sPC[tt % 2].n)
                        r.v.q("tensor_copy", out=piece[:, (tt % 2) * 512:(tt % 2) * 512 + 512],
                              in_=pp).then_inc(ve.inc(), 1)
                        r.y.q("wait_ge", ve.h, ve.n)
                        r.y.q("dma_start",
                              out=x2r_d[th * THW + tt * 128:th * THW + (tt + 1) * 128,
                                        dc * 512:(dc + 1) * 512],
                              in_=piece[:, (tt % 2) * 512:(tt % 2) * 512 + 512]).then_inc(sPC[tt % 2].inc(16), 16)
                        step += 1
            r.y.q("wait_ge", sPC[0].h, sPC[0].n)
            r.y.q("wait_ge", sPC[1].h, sPC[1].n)
            r.y.q("wait_ge", dS.h, dS.n)
        join()
        with ExitStack() as px:
            pf = px.enter_context
            xa = pf(nc.sbuf_tensor("xa", [128, D], F32))
            xb = pf(nc.sbuf_tensor("xb", [128, D], F32))
            xc = pf(nc.sbuf_tensor("xc", [128, D], F32))
            sq4 = pf(nc.sbuf_tensor("sq4", [128, D], F32))
            lnt4 = pf(nc.sbuf_tensor("lnt4", [128, D], BF16))
            mr4 = pf(nc.sbuf_tensor("mr4", [128, 2], F32))
            nr4 = pf(nc.sbuf_tensor("nr4", [128, 8], F32))
            for ti in range(TQ // 128):
                r.y.q("wait_ge", ve.h, ve.n)
                r.y.q("dma_start", out=xa[:, :],
                      in_=x2r_d[ti * 128:(ti + 1) * 128, :]).then_inc(dX.inc(16), 16)
                r.y.q("dma_start", out=xb[:, :],
                      in_=x1q_d[ti * 128:(ti + 1) * 128, :]).then_inc(dX.inc(16), 16)
                r.v.q("wait_ge", dX.h, dX.n)
                r.v.q("tensor_tensor", out=xc[:, :], in0=xa[:, :], in1=xb[:, :],
                      op=OP.add).then_inc(ve.inc(), 1)
                r.y.q("wait_ge", ve.h, ve.n)
                r.y.q("dma_start", out=x2_d[ti * 128:(ti + 1) * 128, :],
                      in_=xc[:, :]).then_inc(dO.inc(16), 16)
                r.v.q("wait_ge", dO.h, dO.n)
                ln_pass(xc[:, :], lnt4[:, :], sq4, mr4, nr4)
                r.v.ops[-1].then_inc(ve.inc(), 1)
                r.y.q("wait_ge", ve.h, ve.n)
                r.y.q("dma_start", out=ln3_d[ti * 128:(ti + 1) * 128, :],
                      in_=lnt4[:, :]).then_inc(dO.inc(16), 16)
            r.y.q("wait_ge", dO.h, dO.n)
        join()

    if n_phases >= 5:
        ffn_call(TQ, x2_d, ln3_d, out_d, ffn1=False)
        r.y.q("wait_ge", dO.h, dO.n)
        join()

    if dev_dump:
        for name, (dst, src, shp) in dbg.items():
            r.g.q("dma_start", out=dst, in_=src.ap()[:, :]).then_inc(dW.inc(16), 16)
        r.g.q("wait_ge", dW.h, dW.n)

    r.y.q("wait_ge", dO.h, dO.n)

    ctx.close()

    with nc.Block() as block:
        @block.tensor
        def _(t):
            r.t.replay(t, REGS)

        @block.vector
        def _(v):
            r.v.replay(v, REGS)

        @block.scalar
        def _(s):
            r.s.replay(s, REGS)

        @block.gpsimd
        def _(g):
            r.g.replay(g, REGS)

        @block.sync
        def _(y):
            r.y.replay(y, REGS)

    nc.compile()
    return nc, dbg


_NAME = {
    "w_dq": "L0_attn_w_dq", "w_uq": "L0_attn_w_uq", "w_dkv": "L0_attn_w_dkv",
    "w_uk": "L0_attn_w_uk", "w_uv": "L0_attn_w_uv", "w_rq": "L0_attn_w_rq",
    "w_rk": "L0_attn_w_rk", "w_out": "L0_attn_w_out",
    "b_uv": "L0_attn_b_uv", "b_out": "L0_attn_b_out", "b2": "L0_ffn_b2",
    "U1": "L0_ffn_U1", "V1": "L0_ffn_V1", "U2": "L0_ffn_U2", "V2": "L0_ffn_V2",
}


def _bcols_from(inputs):
    bc = np.zeros((128, 269), np.float32)
    off = 0

    def put(vec, n, rows=128):
        nonlocal off
        v = np.asarray(vec, np.float32).reshape(n, rows)
        bc[0:rows, off:off + n] = v.T
        off += n

    put(inputs["L0_attn_b_dq"], 12)
    put(inputs["L0_attn_b_uq"], 32)
    put(inputs["L0_attn_b_dkv"], 4)
    put(inputs["L0_attn_b_uk"], 32)
    put(inputs["L0_attn_b_rq"], 16)
    v = np.asarray(inputs["L0_attn_b_rk"], np.float32).reshape(64)
    bc[0:64, off] = v
    off += 1
    b1 = np.asarray(inputs["L0_ffn_b1"], np.float32)
    put(b1[:INTER], 86)
    put(b1[INTER:], 86)
    assert off == 269
    return bc


def make_inmaps(inputs, TH, rng=None):
    S, TQ = 2 * TH, TH // 2
    x = np.ascontiguousarray(np.asarray(inputs["L0_attn_sequence"], np.float32))
    bc = _bcols_from(inputs)
    if rng is None:
        rng = np.random.default_rng()
    nonce = rng.integers(1, 2 ** 30, size=(1, 4)).astype(np.float32)
    common = {"bcols": bc, "nonce": nonce}
    for k, src in _NAME.items():
        a = np.asarray(inputs[src], np.float32)
        if a.ndim == 1:
            a = a.reshape(1, -1)
        common[k] = np.ascontiguousarray(a)
    in_maps = []
    for c in range(8):
        b, w = c // 4, c % 4
        xb = x[b, :S]
        if w == 0:
            xh = xb[0:TH]
        elif w == 1:
            xh = xb[TH:2 * TH]
        elif w == 2:
            xh = np.concatenate([xb[TQ:TH], xb[0:TQ]], 0)
        else:
            xh = np.concatenate([xb[TH + TQ:2 * TH], xb[TH:TH + TQ]], 0)
        in_maps.append({**common, "xh": np.ascontiguousarray(xh)})
    return in_maps


def assemble_out(results, TH, full_shape):
    S, TQ = 2 * TH, TH // 2
    out = np.zeros((2, S, D), np.float32)
    qr = {0: (0, TQ), 1: (TH, TH + TQ), 2: (TQ, TH), 3: (TH + TQ, 2 * TH)}
    for c in range(8):
        b, w = c // 4, c % 4
        lo, hi = qr[w]
        out[b, lo:hi] = np.asarray(results[c]["out"], np.float32)
    return out


def kernel(**inputs):
    TH = 4096
    nc, _ = build_program(TH, n_phases=99, dev_dump=False)
    in_maps = make_inmaps(inputs, TH)
    res = run_bass_kernel_spmd(nc, in_maps, core_ids=list(range(8)))
    return assemble_out(res.results, TH, None)
